# version 06 (final)
# speedup vs baseline: 1.3220x; 1.0183x over previous
import math
from contextlib import ExitStack

import numpy as np
import ml_dtypes
import concourse.bass as bass
import concourse.mybir as mybir
from concourse.bass_utils import run_bass_kernel_spmd

F32 = mybir.dt.float32
BF16 = mybir.dt.bfloat16
AF = mybir.ActivationFunctionType
ALU = mybir.AluOpType
AX = mybir.AxisListType

N_CORES = 8
S = 2048
D = 1024
NB = 4
NH = 16
DH = 64
NEG = -30000.0

ENGINES = ("pe", "act", "dve", "pool", "sp")
ENG_ATTR = {"pe": "tensor", "act": "scalar", "dve": "vector", "pool": "gpsimd", "sp": "sync"}
QUEUES = {"qsp": "sp", "qact": "act", "qgp": "pool"}



class Op:
    __slots__ = ("stream", "issuer", "fn", "deps", "need_sem", "val", "seg",
                 "hazard", "is_dma", "order", "tag", "semkey")

    def __init__(self, stream, issuer, fn, seg, hazard=False, is_dma=False, tag=""):
        self.stream = stream
        self.issuer = issuer
        self.fn = fn
        self.deps = []
        self.need_sem = is_dma
        self.val = None
        self.seg = seg
        self.hazard = hazard
        self.is_dma = is_dma
        self.order = None
        self.tag = tag


class Res:
    __slots__ = ("writer", "readers", "name")

    def __init__(self, name=""):
        self.writer = None
        self.readers = []
        self.name = name


class Prog:
    def __init__(self, nc):
        self.nc = nc
        self.ops = []
        self.seg = 0

    def new_segment(self):
        self.seg += 1

    def _add_dep(self, op, dep):
        if dep is op:
            return
        op.deps.append(dep)

    def emit(self, engine, fn, reads=(), writes=(), hazard=False, tag=""):
        op = Op(engine, engine, fn, self.seg, hazard=hazard, tag=tag)
        for r in reads:
            if r.writer is not None:
                self._add_dep(op, r.writer)
            r.readers.append(op)
        for r in writes:
            if r.writer is not None:
                self._add_dep(op, r.writer)
            for rd in r.readers:
                self._add_dep(op, rd)
            r.writer = op
            r.readers = []
        self.ops.append(op)
        return op

    def dma(self, queue, fn, reads=(), writes=(), tag=""):
        op = Op(queue, QUEUES[queue], fn, self.seg, is_dma=True, tag=tag)
        for r in reads:
            if r.writer is not None:
                self._add_dep(op, r.writer)
            r.readers.append(op)
        for r in writes:
            if r.writer is not None:
                self._add_dep(op, r.writer)
            for rd in r.readers:
                self._add_dep(op, rd)
            r.writer = op
            r.readers = []
        self.ops.append(op)
        return op

    def barrier_all(self):
        r = Res("barrier")
        for e in ENGINES:
            self.emit(e, lambda eng: eng.nop(), reads=(), writes=(r,))
        return r

    def finalize(self, stack):
        nc = self.nc
        for op in self.ops:
            for dep in op.deps:
                if dep.is_dma:
                    dep_needs = True
                elif op.is_dma:
                    dep_needs = True
                elif dep.stream != op.stream:
                    dep_needs = True
                elif dep.hazard:
                    dep_needs = True
                else:
                    dep_needs = False
                if dep_needs:
                    dep.need_sem = True
        sems = {}
        counts = {}
        qidx = {}
        for op in self.ops:
            if op.need_sem:
                if op.is_dma:
                    qi = qidx.get(op.stream, 0)
                    qidx[op.stream] = qi + 1
                    key = (op.stream, qi % 8)
                    op.val = 16 * (qi // 8 + 1)
                    counts[key] = op.val
                else:
                    key = (op.stream, op.seg)
                    counts[key] = counts.get(key, 0) + 1
                    op.val = counts[key]
                op.semkey = key
                if key not in sems:
                    sems[key] = stack.enter_context(
                        nc.semaphore(f"s_{key[0]}_{key[1]}"))
        for key, cnt in counts.items():
            assert cnt < 60000, f"sem overflow risk {key}: {cnt}"
        by_issuer = {e: [] for e in ENGINES}
        for i, op in enumerate(self.ops):
            op.order = i
            by_issuer[op.issuer].append(op)

        with nc.Block() as blk:
            for ename in ENGINES:
                ops = by_issuer[ename]

                def body(eng, ops=ops, ename=ename):
                    guar = {}
                    for op in ops:
                        for dep in op.deps:
                            if dep.val is None:
                                continue
                            if (dep.stream == ename and not dep.is_dma
                                    and not dep.hazard):
                                continue
                            key = dep.semkey
                            if guar.get(key, -1) >= dep.val:
                                continue
                            eng.wait_ge(sems[key], dep.val)
                            guar[key] = dep.val
                        instr = op.fn(eng)
                        if op.need_sem:
                            instr.then_inc(sems[op.semkey], 16 if op.is_dma else 1)

                getattr(blk, ENG_ATTR[ename])(body)
        return sems



def retile(w):
    K, M = w.shape
    assert K % 128 == 0 and M % 128 == 0
    return np.ascontiguousarray(
        w.reshape(K // 128, 128, M // 128, 128).transpose(2, 1, 0, 3))


def rope_tables():
    inv = 1.0 / (10000.0 ** (np.arange(0, DH, 2, dtype=np.float64) / DH))
    pos = np.arange(S, dtype=np.float64)
    ang = pos[None, :] * inv[:, None]
    cos = np.cos(ang).astype(np.float32)
    sin = np.sin(ang).astype(np.float32)
    M1 = np.concatenate([cos] * 4, axis=0)
    M2 = np.concatenate([sin] * 4, axis=0)
    P = np.zeros((128, 128), dtype=np.float32)
    for blk in (0, 64):
        for j in range(32):
            P[blk + 32 + j, blk + j] = -1.0
            P[blk + j, blk + 32 + j] = 1.0
    return M1, M2, P


def rope_perm():
    perm = np.zeros(D, dtype=np.int64)
    i = 0
    for h in range(NH):
        for j in range(32):
            perm[i] = h * 64 + 2 * j
            i += 1
        for j in range(32):
            perm[i] = h * 64 + 2 * j + 1
            i += 1
    return perm


def bf(x):
    return np.ascontiguousarray(x).astype(ml_dtypes.bfloat16)


def host_prep(inp):
    g = {}
    perm = rope_perm()
    g["w_a_q"] = retile(np.ascontiguousarray(inp["L0_attn_a_Wq"].T[:, perm]))
    g["w_a_k"] = retile(np.ascontiguousarray(inp["L0_attn_a_Wk"].T[:, perm]))
    g["w_a_v"] = retile(np.ascontiguousarray(inp["L0_attn_a_Wv"].T))
    g["w_a_o"] = retile(np.ascontiguousarray(inp["L0_attn_a_Wo"].T))
    for nm, w in (("w_b_q", inp["L0_attn_b_Wq"]), ("w_b_k", inp["L0_attn_b_Wk"])):
        g[nm] = retile(np.ascontiguousarray(w.transpose(1, 0, 2).reshape(D, D)))
    g["w_b_v"] = retile(np.ascontiguousarray(
        inp["L0_attn_b_Wv"].transpose(1, 0, 2).reshape(D, D)))
    wp = inp["L0_attn_b_Wp"].reshape(DH, NH, D).transpose(1, 0, 2).reshape(D, D)
    g["w_b_p"] = retile(np.ascontiguousarray(wp))
    g["w_f_router"] = retile(np.ascontiguousarray(
        np.pad(inp["L0_ffn_router_w"], ((0, 0), (0, 124)))))
    g["w_f_sw1"] = retile(inp["L0_ffn_sw_w1"])
    g["w_f_sw3"] = retile(inp["L0_ffn_sw_w3"])
    g["w_f_sw2"] = retile(inp["L0_ffn_sw_w2"])
    g["w_f_eq"] = retile(inp["L0_ffn_me_eq_w"])
    g["w_f_mwv"] = retile(inp["L0_ffn_me_wv"])
    g["w_f_mwo"] = retile(inp["L0_ffn_me_wo"])
    g["w_f_c1"] = retile(inp["L0_ffn_me_c1w"])
    g["w_f_c2"] = retile(inp["L0_ffn_me_c2w"])
    g["w_f_syn"] = retile(inp["L0_ffn_ce_syn_w"])
    g["w_f_cwv"] = retile(inp["L0_ffn_ce_wv"])
    g["w_f_cwo"] = retile(inp["L0_ffn_ce_wo"])
    g["w_f_f1"] = retile(inp["L0_ffn_ce_f1w"])
    g["w_f_f2"] = retile(inp["L0_ffn_ce_f2w"])
    g["w_f_gen"] = retile(inp["L0_ffn_ce_gen_w"])
    g["w_f_ml1"] = retile(inp["L0_ffn_ml_w1"])
    g["w_f_ml2"] = retile(inp["L0_ffn_ml_w2"])
    wq = np.ascontiguousarray(inp["L1_attn_a_Wqkv"].T)
    g["w_la_q"] = retile(np.ascontiguousarray(wq[:, 0:1024]))
    g["w_la_kv"] = retile(np.ascontiguousarray(wq[:, 1024:3072]))
    g["w_la_p"] = retile(np.ascontiguousarray(inp["L1_attn_a_Wproj"].T))
    g["w_lf_s1"] = retile(inp["L1_ffn_sw1"])
    g["w_lf_s3"] = retile(inp["L1_ffn_sw3"])
    g["w_lf_s2"] = retile(inp["L1_ffn_sw2"])
    for e in range(8):
        g[f"w_lf_e1_{e}"] = retile(np.ascontiguousarray(inp["L1_ffn_w1"][e]))
        g[f"w_lf_e2_{e}"] = retile(np.ascontiguousarray(inp["L1_ffn_w2"][e]))
    g["w_lb_q"] = retile(np.ascontiguousarray(inp["L1_attn_b_Wq"].T))
    g["w_lb_k"] = retile(np.ascontiguousarray(inp["L1_attn_b_Wk"].T))
    g["w_lb_v"] = retile(np.ascontiguousarray(inp["L1_attn_b_Wv"].T))
    g["w_lb_o"] = retile(np.ascontiguousarray(inp["L1_attn_b_Wo"].T))
    M1, M2, _ = rope_tables()
    g["c_M1"] = M1
    g["c_M2"] = M2
    return g


def host_consts():
    c = {}
    _, _, P = rope_tables()
    c["c_P"] = P
    c["c_ident"] = np.eye(128, dtype=np.float32)
    c["c_identb"] = np.eye(128, dtype=np.float32)
    r = np.arange(128)
    c["c_mask_causal"] = (r[None, :] >= r[:, None]).astype(np.float32)
    c["c_mask_wind"] = (r[None, :] < r[:, None]).astype(np.float32)
    c["c_ones_col"] = np.ones((128, 1), dtype=np.float32)
    c["c_ones_row"] = np.ones((1, 128), dtype=np.float32)
    pairs = [(a, b) for b in range(4) for a in range(4) if a != b]
    Dm = np.zeros((4, 12), dtype=np.float32)
    Am = np.zeros((12, 4), dtype=np.float32)
    for p, (a, b) in enumerate(pairs):
        Dm[a, p] += 1.0
        Dm[b, p] -= 1.0
        Am[p, b] = 1.0
    c["c_Dmat"] = Dm
    c["c_Amat"] = Am
    sel = np.zeros((4, 4, 128), dtype=np.float32)
    for e in range(4):
        sel[e, e, :] = 1.0
    c["c_sel4"] = sel.reshape(4, 4 * 128)
    return c



def cdiv(a, b):
    return (a + b - 1) // b



class Build:
    def __init__(self, nc, stack, gshapes, n_steps=8, nb=NB):
        self.nc = nc
        self.stack = stack
        self.P = Prog(nc)
        self.n_steps = n_steps
        self.nb = nb
        ent = stack.enter_context

        self.x_in = nc.dram_tensor("x", [nb, S, D], F32, kind="ExternalInput").ap()
        self.x_out = nc.dram_tensor("y", [nb, S, D], F32, kind="ExternalOutput").ap()
        self.wf32 = {}
        self.wbf = {}
        self.r_wbf = {}
        for name, shp in gshapes.items():
            self.wf32[name] = nc.dram_tensor(name, list(shp), F32, kind="ExternalInput").ap()
            self.wbf[name] = nc.dram_tensor("b_" + name, list(shp), BF16, kind="Internal").ap()
            self.r_wbf[name] = Res("wbf_" + name)
        self.vecs = {}

        self.XR = ent(nc.sbuf_tensor("XR", [128, 8 * S], F32))
        self.XN = ent(nc.sbuf_tensor("XN", [128, 8 * S], BF16))
        self.BB1 = ent(nc.sbuf_tensor("BB1", [128, 16384], BF16))
        self.BB2 = ent(nc.sbuf_tensor("BB2", [128, 16640], BF16))
        self.ATT = ent(nc.sbuf_tensor("ATT", [128, 8192], BF16))
        self.SC32 = ent(nc.sbuf_tensor("SC32", [128, 512], F32))
        self.WS = ent(nc.sbuf_tensor("WS", [128, 6144], BF16))
        self.ROWS = ent(nc.sbuf_tensor("ROWS", [128, 2048], F32))
        self.ROWSB = ent(nc.sbuf_tensor("ROWSB", [128, 2048], BF16))
        self.COLS = ent(nc.sbuf_tensor("COLS", [128, 384], F32))
        self.CB = ent(nc.sbuf_tensor("CB", [128, 1302], BF16))
        self.CF = ent(nc.sbuf_tensor("CF", [128, 146], F32))
        self.PS = ent(nc.psum_tensor("PS", [128, 4096], F32))

        self.r_xr = [Res(f"xr{c}") for c in range(8)]
        self.r_xn = [Res(f"xn{c}") for c in range(8)]
        self.r_bb1 = Res("bb1")
        self.r_bb2 = Res("bb2")
        self.r_att = [Res(f"att{i}") for i in range(4)]
        self.r_wb = Res("wb")
        self.r_tmp0 = Res("tmp0")
        self.r_tmp1 = Res("tmp1")
        self.r_mid = (self.r_wb, self.r_tmp0, self.r_tmp1)
        self.r_sc32 = Res("sc32")
        self.r_ws = [Res(f"ws{i}") for i in range(6)]
        self.r_rows = Res("rows")
        self.r_rowsb = Res("rowsb")
        self.r_cols = Res("cols")
        self.r_bank = [Res(f"bank{i}") for i in range(8)]
        self.r_bnd = Res("bnd")
        self.r_linv2 = Res("linv2")
        self.r_out = Res("out_dma")
        self._col_off = 0
        self._cf_off = 128
        self._cb_off = 0
        self.col_map = {}
        self.debug = False
        self.dbg_names = []
        self.dbg_ap = None

    def dump(self, name, ap, res, bf16=False):
        if not self.debug:
            return
        if self.dbg_ap is None:
            self.dbg_ap = self.nc.dram_tensor("dbg", [64, 128, 2048], F32,
                                              kind="ExternalOutput").ap()
        i = len(self.dbg_names)
        self.dbg_names.append(name)
        rows, cols = ap.shape
        dst = self.dbg_ap[i, 0:rows, 0:cols]
        q = "qgp" if bf16 else "qact"
        if res is None:
            rr = ()
        elif isinstance(res, tuple):
            rr = res
        else:
            rr = (res,)
        self.P.dma(q, lambda eng, dst=dst, ap=ap: eng.dma_start(dst, ap),
                   reads=rr, writes=(self.r_out,), tag="dbg_" + name)

    def alloc_cols(self, name, n):
        sl = (self._col_off, self._col_off + n)
        self._col_off += n
        assert self._col_off <= 372
        self.col_map[name] = sl
        return sl

    def col(self, name, j=0):
        a, b = self.col_map[name]
        assert a + j < b
        return self.COLS[:, a + j:a + j + 1]

    def pbank(self, b, lo=0, hi=512, rows=(0, 128)):
        return self.PS[rows[0]:rows[1], b * 512 + lo:b * 512 + hi]

    def pspan(self, b0, nb_, rows=(0, 128)):
        return self.PS[rows[0]:rows[1], b0 * 512:(b0 + nb_) * 512]

    def prologue(self, consts, vec_names):
        P, nc = self.P, self.nc
        for name, ap in self.wf32.items():
            flat_in = ap.rearrange(" ".join(f"a{i}" for i in range(len(ap.shape)))
                                   + " -> (" + " ".join(f"a{i}" for i in range(len(ap.shape))) + ")")
            flat_out = self.wbf[name].rearrange(
                " ".join(f"a{i}" for i in range(len(ap.shape)))
                + " -> (" + " ".join(f"a{i}" for i in range(len(ap.shape))) + ")")
            n = flat_in.shape[0]
            step = 1 << 20
            for off in range(0, n, step):
                e = min(off + step, n)
                P.dma("qgp", lambda eng, o=off, e_=e, fi=flat_in, fo=flat_out:
                      eng.dma_start(fo[o:e_], fi[o:e_]),
                      writes=(self.r_wbf[name],), tag="wconv")
        self.cmap = {}
        for name, arr in consts.items():
            self.cmap[name] = nc.dram_tensor(name, list(arr.shape), F32,
                                             kind="ExternalInput").ap()
        P.dma("qact", lambda eng: eng.dma_start(self.CF[:, 0:128], self.cmap["c_ident"]),
              writes=(self.r_cols,))
        P.dma("qact", lambda eng: eng.dma_start(self.CF[0:4, 134:146],
                                                self.cmap["c_Dmat"]),
              writes=(self.r_cols,))
        self.cb_map = {}
        for name, ncols in (("c_mask_causal", 128), ("c_mask_wind", 128), ("c_P", 128),
                            ("c_Dmat", 12), ("c_Amat", 4), ("c_sel4", 512),
                            ("c_identb", 128)):
            off = self._cb_off
            self._cb_off += ncols
            self.cb_map[name] = off
            P.dma("qgp", lambda eng, nm=name, o=off, w=ncols:
                  eng.dma_start(self.CB[0:self.cmap[nm].shape[0], o:o + w], self.cmap[nm]),
                  writes=(self.r_cols,))
        P.emit("dve", lambda eng: eng.memset(self.CF[:, 128:130], 1.0), writes=(self.r_cols,))
        P.emit("dve", lambda eng: eng.memset(self.CF[:, 130:131], 1e-5), writes=(self.r_cols,))
        P.emit("dve", lambda eng: eng.memset(self.CF[:, 131:132], -30.0), writes=(self.r_cols,))
        off = self._cb_off
        self.cb_map["ones"] = off
        self._cb_off += 130
        P.emit("dve", lambda eng, o=off: eng.memset(self.CB[:, o:o + 130], 1.0),
               writes=(self.r_cols,))
        off = self._cb_off
        self.cb_map["negones"] = off
        self._cb_off += 128
        P.emit("dve", lambda eng, o=off: eng.memset(self.CB[:, o:o + 128], -1.0),
               writes=(self.r_cols,))
        self.vec_aps = {}
        for vn, src_name, length in vec_names:
            self.vec_aps[vn] = nc.dram_tensor(
                "v_" + vn, [length], F32, kind="ExternalInput").ap()
            if length < 128:
                self.alloc_cols(vn, 1)
                a, _ = self.col_map[vn]
                P.dma("qact", lambda eng, vn=vn, length=length:
                      eng.dma_start(self.ROWS[0:1, 0:length],
                                    self.vec_aps[vn][:][None, :]),
                      writes=(self.r_rows,))
                P.emit("pe", lambda eng, length=length:
                       eng.matmul(self.pbank(0, 0, 1, rows=(0, length)),
                                  self.ROWS[0:1, 0:length],
                                  self.CF[0:1, 128:129], start=True, stop=True),
                       reads=(self.r_rows, self.r_cols), writes=(self.r_bank[0],))
                P.emit("dve", lambda eng, a=a, length=length:
                       eng.tensor_copy(self.COLS[0:length, a:a + 1],
                                       self.pbank(0, 0, 1, rows=(0, length))),
                       reads=(self.r_bank[0],), writes=(self.r_cols,))
                continue
            nchunk = length // 128
            self.alloc_cols(vn, nchunk)
            a, _ = self.col_map[vn]
            for part in range(cdiv(length, 2048)):
                lo = part * 2048
                hi = min(lo + 2048, length)
                P.dma("qact", lambda eng, lo=lo, hi=hi, vn=vn:
                      eng.dma_start(self.ROWS[0:1, 0:hi - lo],
                                    self.vec_aps[vn][lo:hi][None, :]),
                      writes=(self.r_rows,))
                for j in range((hi - lo) // 128):
                    cidx = (lo // 128) + j
                    bk = cidx % 2
                    P.emit("pe", lambda eng, j=j, bk=bk:
                           eng.matmul(self.pbank(bk, 0, 1), self.ROWS[0:1, j * 128:(j + 1) * 128],
                                      self.CF[0:1, 128:129], start=True, stop=True),
                           reads=(self.r_rows, self.r_cols), writes=(self.r_bank[bk],))
                    P.emit("dve", lambda eng, a=a, cidx=cidx, bk=bk:
                           eng.tensor_copy(self.COLS[:, a + cidx:a + cidx + 1],
                                           self.pbank(bk, 0, 1)),
                           reads=(self.r_bank[bk],), writes=(self.r_cols,))
        self.alloc_cols("rblb", 1)
        self.alloc_cols("lfb2c", 8)
        self.dump("cols", self.COLS[:, 0:384], self.r_cols)
        arr = self.col_map["rblb"][0]
        P.emit("dve", lambda eng, arr=arr:
               eng.tensor_add(self.COLS[0:4, arr:arr + 1],
                              self.COLS[0:4, self.col_map["rb"][0]:
                                        self.col_map["rb"][0] + 1],
                              self.COLS[0:4, self.col_map["lb"][0]:
                                        self.col_map["lb"][0] + 1]),
               reads=(self.r_cols,), writes=(self.r_cols,))

    def load_x(self, b):
        P = self.P
        for st in range(16):
            P.dma("qact", lambda eng, st=st, b=b:
                  eng.dma_start(self.ROWS[:, 0:1024],
                                self.x_in[b, st * 128:(st + 1) * 128, :]),
                  writes=(self.r_rows,), tag="xin")
            for c in range(8):
                P.emit("pe", lambda eng, c=c:
                       eng.transpose(self.PS[:, c * 128:(c + 1) * 128],
                                     self.ROWS[:, c * 128:(c + 1) * 128],
                                     self.CF[:, 0:128]),
                       reads=(self.r_rows, self.r_cols),
                       writes=(self.r_bank[c // 4],))
            dst = self.XR[:, :].rearrange("p (c s) -> p c s", c=8)[:, :, st * 128:(st + 1) * 128]
            src = self.PS[:, 0:1024].rearrange("p (c s) -> p c s", c=8)
            P.emit("dve", lambda eng, dst=dst, src=src: eng.tensor_copy(dst, src),
                   reads=(self.r_bank[0], self.r_bank[1]),
                   writes=tuple(self.r_xr), tag="xin_evac")

    def store_x(self, b):
        P = self.P
        for st in range(16):
            for c in range(8):
                P.emit("pe", lambda eng, c=c, st=st:
                       eng.transpose(self.PS[:, 1024 + c * 128:1024 + (c + 1) * 128],
                                     self.XR[:, c * S + st * 128:c * S + (st + 1) * 128],
                                     self.CF[:, 0:128]),
                       reads=(self.r_xr[c], self.r_cols),
                       writes=(self.r_bank[2 + c // 4],))
            src = self.PS[:, 1024:2048].rearrange("p (c s) -> p c s", c=8)
            P.emit("dve", lambda eng, src=src:
                   eng.tensor_copy(self.ROWS[:, 0:1024].rearrange("p (c s) -> p c s", c=8),
                                   src),
                   reads=(self.r_bank[2], self.r_bank[3]), writes=(self.r_rows,))
            P.dma("qact", lambda eng, st=st, b=b:
                  eng.dma_start(self.x_out[b, st * 128:(st + 1) * 128, :],
                                self.ROWS[:, 0:1024]),
                  reads=(self.r_rows,), writes=(self.r_out,), tag="xout")

    def finish(self):
        self.P.emit("sp", lambda eng: eng.nop(), reads=(self.r_out,))

    def phase_fence(self):
        rs = tuple(self.r_att) + self.r_mid + (self.r_sc32, self.r_rows,
                                               self.r_rowsb, self.r_linv2)
        self.P.emit("dve", lambda eng: eng.nop(), reads=(), writes=rs, tag="fence")

    def load_strip(self, wname, mc, slot, n_mc=1):
        src = self.wbf[wname]
        MC, _, KC, _ = src.shape
        ncols = n_mc * KC * 128
        off = slot * 1024
        assert ncols <= 1024 * (6 - slot) if False else True
        view = src[mc:mc + n_mc].rearrange("m p k c -> p (m k c)")
        slots = [self.r_ws[slot + i] for i in range(cdiv(ncols, 1024))]
        self.P.dma("qsp", lambda eng, v=view, o=off, n=ncols:
                   eng.dma_start(self.WS[:, o:o + n], v),
                   reads=(self.r_wbf[wname],), writes=tuple(slots), tag="wld")
        return self.WS[:, off:off + ncols], slots

    def load_strip_part(self, wname, mc, kc0, kc1, slot):
        src = self.wbf[wname]
        view = src[mc:mc + 1].rearrange("m p k c -> p (m k c)")[
            :, kc0 * 128:kc1 * 128]
        ncols = (kc1 - kc0) * 128
        off = slot * 1024
        slots = [self.r_ws[slot + i] for i in range(cdiv(ncols, 1024))]
        self.P.dma("qsp", lambda eng, v=view, o=off, n=ncols:
                   eng.dma_start(self.WS[:, o:o + n], v),
                   reads=(self.r_wbf[wname],), writes=tuple(slots), tag="wldp")
        return self.WS[:, off:off + ncols], slots

    def ln(self, in_chunks, out_chunks, in_res, out_res, width, n_chunks=8,
           gamma=None, beta=None, in_f32=True, scratch=None, keep_m=False):
        P = self.P
        nseg = cdiv(width, 512)
        dmodel = float(n_chunks * 128)
        sq, r_sc = scratch
        if not isinstance(r_sc, tuple):
            r_sc = (r_sc,)
        onb = self.cb_map["ones"]
        ones_col = self.CB[:, onb + 128:onb + 129]
        ones_row = self.CB[0:1, onb:onb + 128]
        for c in range(n_chunks):
            if in_f32:
                P.emit("dve", lambda eng, c=c:
                       eng.tensor_copy(sq, in_chunks[c]),
                       reads=(in_res[c],), writes=r_sc, tag="ln_xbf")
                rhs, rres = sq, r_sc
            else:
                rhs, rres = in_chunks[c], (in_res[c],)
            for sg in range(nseg):
                lo, hi = sg * 512, min((sg + 1) * 512, width)
                P.emit("pe", lambda eng, c=c, lo=lo, hi=hi, sg=sg, rhs=rhs:
                       eng.matmul(self.PS[0:1, sg * 512:sg * 512 + (hi - lo)],
                                  ones_col, rhs[:, lo:hi],
                                  start=(c == 0), stop=(c == n_chunks - 1)),
                       reads=rres + (self.r_cols,), writes=(self.r_bank[sg],), tag="ln_s")
        sbanks = tuple(self.r_bank[i] for i in range(nseg))
        mbanks = tuple(self.r_bank[4 + i] for i in range(nseg))
        P.emit("dve", lambda eng:
               eng.tensor_scalar_mul(self.ROWSB[0:1, 0:width], self.PS[0:1, 0:width],
                                     1.0 / dmodel),
               reads=sbanks, writes=(self.r_rowsb,), tag="ln_m")
        if keep_m:
            P.emit("dve", lambda eng:
                   eng.tensor_scalar_mul(self.ROWS[0:1, 0:width],
                                         self.PS[0:1, 0:width], 1.0 / dmodel),
                   reads=sbanks, writes=(self.r_rows,), tag="ln_mf")
        for sg in range(nseg):
            lo, hi = sg * 512, min((sg + 1) * 512, width)
            P.emit("pe", lambda eng, lo=lo, hi=hi, sg=sg:
                   eng.matmul(self.PS[:, 2048 + sg * 512:2048 + sg * 512 + (hi - lo)],
                              ones_row, self.ROWSB[0:1, lo:hi], start=True, stop=True),
                   reads=(self.r_rowsb, self.r_cols), writes=(self.r_bank[4 + sg],),
                   tag="ln_mb")
        for c in range(n_chunks):
            P.emit("dve", lambda eng, c=c:
                   eng.tensor_sub(out_chunks[c], in_chunks[c], self.PS[:, 2048:2048 + width]),
                   reads=(in_res[c],) + mbanks, writes=(out_res[c],), tag="ln_xc")
        for c in range(n_chunks):
            P.emit("act", lambda eng, c=c:
                   eng.activation(sq, out_chunks[c], AF.Square),
                   reads=(out_res[c],), writes=r_sc, tag="ln_sq")
            for sg in range(nseg):
                lo, hi = sg * 512, min((sg + 1) * 512, width)
                P.emit("pe", lambda eng, c=c, lo=lo, hi=hi, sg=sg:
                       eng.matmul(self.PS[0:1, sg * 512:sg * 512 + (hi - lo)],
                                  ones_col, sq[:, lo:hi],
                                  start=(c == 0), stop=(c == n_chunks - 1)),
                       reads=r_sc + (self.r_cols,), writes=(self.r_bank[sg],), tag="ln_q")
        P.emit("act", lambda eng:
               eng.activation(self.ROWSB[0:1, 0:width], self.PS[0:1, 0:width],
                              AF.Ln, bias=self.CF[0:1, 130:131], scale=1.0 / dmodel),
               reads=sbanks + (self.r_cols,), writes=(self.r_rowsb,), tag="ln_ln")
        P.emit("act", lambda eng:
               eng.activation(self.ROWSB[0:1, 0:width], self.ROWSB[0:1, 0:width],
                              AF.Exp, scale=-0.5),
               reads=(self.r_rowsb,), writes=(self.r_rowsb,), tag="ln_exp")
        for sg in range(nseg):
            lo, hi = sg * 512, min((sg + 1) * 512, width)
            P.emit("pe", lambda eng, lo=lo, hi=hi, sg=sg:
                   eng.matmul(self.PS[:, 2048 + sg * 512:2048 + sg * 512 + (hi - lo)],
                              ones_row, self.ROWSB[0:1, lo:hi], start=True, stop=True),
                   reads=(self.r_rowsb, self.r_cols), writes=(self.r_bank[4 + sg],),
                   tag="ln_rb")
        for c in range(n_chunks):
            P.emit("dve", lambda eng, c=c:
                   eng.tensor_mul(out_chunks[c], out_chunks[c],
                                  self.PS[:, 2048:2048 + width]),
                   reads=(out_res[c],) + mbanks, writes=(out_res[c],), tag="ln_xn")
            if gamma is not None:
                P.emit("act", lambda eng, c=c:
                       eng.activation(out_chunks[c], out_chunks[c], AF.Identity,
                                      bias=self.col(beta, c), scale=self.col(gamma, c)),
                       reads=(out_res[c], self.r_cols), writes=(out_res[c],),
                       tag="ln_gb")

    def ln_pre(self, b):
        ic = [self.XR[:, c * S:(c + 1) * S] for c in range(8)]
        oc = [self.XN[:, c * S:(c + 1) * S] for c in range(8)]
        scratch = (self.ATT[:, 4096:6144], self.r_mid)
        self.ln(ic, oc, self.r_xr, self.r_xn, S, 8, scratch=scratch,
                keep_m=getattr(self, "_keep_m", False))
        self.dump("xn0", self.XN[:, 0:S], self.r_xn[0], bf16=True)
        self.dump("xn7", self.XN[:, 7 * S:8 * S], self.r_xn[7], bf16=True)

    def mm_to_psum(self, strip, strip_res, rhs_chunks, rhs_res, banks0, width,
                   rows=(0, 128), kcs=8, tag="mm"):
        P = self.P
        wbanks = []
        for sg in range(cdiv(width, 512)):
            lo, hi = sg * 512, min((sg + 1) * 512, width)
            bk = banks0 + sg
            wbanks.append(self.r_bank[bk])
            for kc in range(kcs):
                P.emit("pe", lambda eng, kc=kc, lo=lo, hi=hi, bk=bk:
                       eng.matmul(self.PS[rows[0]:rows[1],
                                          bk * 512:bk * 512 + (hi - lo)],
                                  strip[:, kc * 128:(kc + 1) * 128],
                                  rhs_chunks[kc][:, lo:hi],
                                  start=(kc == 0), stop=(kc == kcs - 1)),
                       reads=(strip_res, rhs_res[kc]), tag=tag,
                       writes=(self.r_bank[bk],))
        return wbanks

    def tok_transpose(self, src_bf, src_res, dst_groups, dst_res, n_heads2,
                      group_w, head0, tag="tt"):
        P = self.P
        for st in range(16):
            srt = src_res if isinstance(src_res, tuple) else (src_res,)
            P.emit("dve", lambda eng, st=st:
                   eng.tensor_copy(self.SC32[:, 0:128], src_bf[:, st * 128:(st + 1) * 128]),
                   reads=srt, writes=(self.r_sc32,))
            P.emit("pe", lambda eng, st=st:
                   eng.transpose(self.PS[:, 3584:3712], self.SC32[:, 0:128],
                                 self.CF[:, 0:128]),
                   reads=(self.r_sc32, self.r_cols), writes=(self.r_bank[7],))
            dst = dst_groups(st, head0)
            P.emit("dve", lambda eng, st=st, dst=dst:
                   eng.tensor_copy(dst, self.PS[:, 3584:3712]
                                   .rearrange("p (g c) -> p g c", g=2)),
                   reads=(self.r_bank[7],), writes=(dst_res,))

    def proj_residual(self, wname, src_chunks, src_res, bias=None, scale=None,
                      n_out=8, kcs=8, slot_list=(4, 5), tag="wo"):
        P = self.P
        for mc in range(n_out):
            slot = slot_list[mc % len(slot_list)]
            strip, sres = self.load_strip(wname, mc, slot)
            for sg in range(4):
                bk = sg % 2
                lo = sg * 512
                for kc in range(kcs):
                    P.emit("pe", lambda eng, kc=kc, lo=lo, bk=bk, strip=strip:
                           eng.matmul(self.PS[:, bk * 512:bk * 512 + 512],
                                      strip[:, kc * 128:(kc + 1) * 128],
                                      src_chunks[kc][:, lo:lo + 512],
                                      start=(kc == 0), stop=(kc == kcs - 1)),
                           reads=(sres[0], src_res[kc]), writes=(self.r_bank[bk],),
                           tag=tag)
                xr = self.XR[:, mc * S + lo:mc * S + lo + 512]
                if bias is not None:
                    P.emit("dve", lambda eng, xr=xr, bk=bk, mc=mc:
                           eng.scalar_tensor_tensor(xr, self.PS[:, bk * 512:(bk + 1) * 512],
                                                    self.col(bias, mc), xr,
                                                    ALU.add, ALU.add),
                           reads=(self.r_bank[bk], self.r_cols, self.r_xr[mc]),
                           writes=(self.r_xr[mc],), tag=tag + "_ev")
                else:
                    P.emit("dve", lambda eng, xr=xr, bk=bk:
                           eng.tensor_add(xr, xr, self.PS[:, bk * 512:(bk + 1) * 512]),
                           reads=(self.r_bank[bk], self.r_xr[mc]),
                           writes=(self.r_xr[mc],), tag=tag + "_ev")

    def attn_sm(self, b, wq, wk, wv, wo, n_kv=16, rope=False, qscale=1.0,
                kind="causal", softcap=None, out_bias=None, name="att"):
        P = self.P
        onb = self.cb_map["ones"]
        gw = n_kv * 65
        P.emit("dve", lambda eng: eng.memset(self.BB2[:, 0:16 * gw], 1.0),
               writes=(self.r_bb2,), tag=name + "_vms")
        xn_chunks = [self.XN[:, c * S:(c + 1) * S] for c in range(8)]
        nvc = n_kv * 64 // 128
        for mc in range(nvc):
            strip, sres = self.load_strip(wv, mc, 4 + (mc % 2))
            self.mm_to_psum(strip, sres[0], xn_chunks, self.r_xn, 0, S, tag=name + "_v")
            vstage = self.ATT[:, 4096:6144]
            for sg in range(4):
                P.emit("act", lambda eng, sg=sg, vstage=vstage:
                       eng.activation(vstage[:, sg * 512:(sg + 1) * 512],
                                      self.PS[:, sg * 512:(sg + 1) * 512], AF.Copy),
                       reads=(self.r_bank[sg],), writes=self.r_mid)

            def dstf(st, mc=mc):
                base = st * gw + (2 * mc) * 65
                return self.BB2[:, base:base + 130].rearrange(
                    "p (g c) -> p g c", g=2)[:, :, 0:64]
            self.tok_transpose_bf(vstage, self.r_mid, dstf, self.r_bb2,
                                  tag=name + "_vt")
        if n_kv == 4:
            kbase = 16 * gw
            kch = [self.BB2[:, kbase + i * 2048:kbase + (i + 1) * 2048]
                   for i in range(4)]
            for mc in range(2):
                strip, sres = self.load_strip(wk, mc, 4 + (mc % 2))
                self.mm_to_psum(strip, sres[0], xn_chunks, self.r_xn, 0, S, tag=name + "_k")
                direct = kch[2 * mc]
                swapped = kch[2 * mc + 1]
                for sg in range(4):
                    P.emit("act", lambda eng, sg=sg, direct=direct:
                           eng.activation(direct[:, sg * 512:(sg + 1) * 512],
                                          self.PS[:, sg * 512:(sg + 1) * 512],
                                          AF.Copy),
                           reads=(self.r_bank[sg],), writes=(self.r_bb2,),
                           tag=name + "_kev")
                    P.emit("dve", lambda eng, sg=sg, swapped=swapped:
                           eng.tensor_copy(swapped[0:64, sg * 512:(sg + 1) * 512],
                                           self.PS[64:128, sg * 512:(sg + 1) * 512]),
                           reads=(self.r_bank[sg],), writes=(self.r_bb2,),
                           tag=name + "_kswA")
                    P.emit("dve", lambda eng, sg=sg, swapped=swapped:
                           eng.tensor_copy(swapped[64:128, sg * 512:(sg + 1) * 512],
                                           self.PS[0:64, sg * 512:(sg + 1) * 512]),
                           reads=(self.r_bank[sg],), writes=(self.r_bb2,),
                           tag=name + "_kswB")
        self.dump(name + "_vtok", self.BB2[:, 0:2048], self.r_bb2, bf16=True)
        for p in range(8):
            qstrip, qres = self.load_strip(wq, p, 4)
            if n_kv == 16:
                kstrip, kres = self.load_strip(wk, p, 5)
            self.mm_to_psum(qstrip, qres[0], xn_chunks, self.r_xn, 0, S, tag=name + "_q")
            qdst = self.ATT[:, 0:2048]
            self._evac_qk(qdst, self.r_att[0], rope, qscale, name + "_qe")
            if p == 0:
                self.dump(name + "_q0", qdst, self.r_att[0], bf16=True)
            if n_kv == 16:
                self.mm_to_psum(kstrip, kres[0], xn_chunks, self.r_xn, 0, S, tag=name + "_k")
                kdst = self.ATT[:, 2048:4096]
                self._evac_qk(kdst, self.r_att[1], rope, 1.0, name + "_ke")
                if p == 0:
                    self.dump(name + "_k0", kdst, self.r_att[1], bf16=True)
                if softcap is None:
                    pts16 = [(self.ATT[:, 4096:6144], self.r_mid),
                             (self.ATT[:, 6144:8192], (self.r_att[3],))]
                    for hh2 in range(2):
                        self._bound(hh2, qdst[hh2 * 64:(hh2 + 1) * 64, :],
                                    kdst[hh2 * 64:(hh2 + 1) * 64, :],
                                    (self.r_att[1],), pts16, name)
            for hh in range(2):
                H = 2 * p + hh
                if n_kv == 16:
                    kap = self.ATT[:, 2048:4096][hh * 64:(hh + 1) * 64, :]
                    kres_h = (self.r_att[1],)
                    pts = [(self.ATT[:, 4096:6144], self.r_mid),
                           (self.ATT[:, 6144:8192], (self.r_att[3],))]
                else:
                    kvh = H // 4
                    base = hh * 64
                    if base == 0:
                        ci = [0, 1, 2, 3][kvh]
                    else:
                        ci = [1, 0, 3, 2][kvh]
                    kbase = 16 * gw
                    kap = self.BB2[:, kbase + ci * 2048:
                                   kbase + (ci + 1) * 2048][base:base + 64, :]
                    kres_h = (self.r_bb2,)
                    pts = [(self.ATT[:, 4096:6144], self.r_mid),
                           (self.ATT[:, 6144:8192], (self.r_att[3],))]
                qh = self.ATT[:, 0:2048][hh * 64:(hh + 1) * 64, :]
                self._head(b, H, qh, kap, kres_h, kind, softcap, gw, n_kv, pts, name)
        self.dump(name + "_ao0", self.BB1[:, 0:S], self.r_bb1, bf16=True)
        bb1_chunks = [self.BB1[:, c * S:(c + 1) * S] for c in range(8)]
        self.proj_residual(wo, bb1_chunks, [self.r_bb1] * 8, bias=out_bias,
                           tag=name + "_wo")

    def _evac_qk(self, dst, dst_res, rope, scale, tag):
        P = self.P
        if not rope:
            for sg in range(4):
                if scale == 1.0:
                    P.emit("dve", lambda eng, sg=sg, dst=dst:
                           eng.tensor_copy(dst[:, sg * 512:(sg + 1) * 512],
                                           self.PS[:, sg * 512:(sg + 1) * 512]),
                           reads=(self.r_bank[sg],), writes=(dst_res,), tag=tag)
                else:
                    P.emit("dve", lambda eng, sg=sg, dst=dst:
                           eng.tensor_scalar_mul(dst[:, sg * 512:(sg + 1) * 512],
                                                 self.PS[:, sg * 512:(sg + 1) * 512],
                                                 float(scale)),
                           reads=(self.r_bank[sg],), writes=(dst_res,), tag=tag)
            return
        qraw = self.ATT[:, 4096:6144]
        for sg in range(4):
            P.emit("act", lambda eng, sg=sg:
                   eng.activation(qraw[:, sg * 512:(sg + 1) * 512],
                                  self.PS[:, sg * 512:(sg + 1) * 512], AF.Copy,
                                  scale=scale),
                   reads=(self.r_bank[sg],), writes=self.r_mid, tag=tag + "r")
        pmat = self.CB[:, self.cb_map["c_P"]:self.cb_map["c_P"] + 128]
        for sg in range(4):
            P.emit("pe", lambda eng, sg=sg:
                   eng.matmul(self.PS[:, sg * 512:(sg + 1) * 512], pmat,
                              qraw[:, sg * 512:(sg + 1) * 512], start=True, stop=True),
                   reads=self.r_mid + (self.r_cols,), writes=(self.r_bank[sg],),
                   tag=tag + "sh")
            P.emit("pool", lambda eng, sg=sg, dst=dst:
                   eng.tensor_mul(dst[:, sg * 512:(sg + 1) * 512],
                                  qraw[:, sg * 512:(sg + 1) * 512],
                                  self.WS[:, sg * 512:(sg + 1) * 512]),
                   reads=self.r_mid + (self.r_ws[0], self.r_ws[1]),
                   writes=(dst_res,), tag=tag + "m1")
            P.emit("dve", lambda eng, sg=sg:
                   eng.tensor_mul(self.SC32[:, 0:512],
                                  self.PS[:, sg * 512:(sg + 1) * 512],
                                  self.WS[:, 2048 + sg * 512:2048 + (sg + 1) * 512]),
                   reads=(self.r_bank[sg], self.r_ws[2], self.r_ws[3]),
                   writes=(self.r_sc32,), tag=tag + "m2")
            P.emit("dve", lambda eng, sg=sg, dst=dst:
                   eng.tensor_add(dst[:, sg * 512:(sg + 1) * 512],
                                  dst[:, sg * 512:(sg + 1) * 512], self.SC32[:, 0:512]),
                   reads=(self.r_sc32, dst_res), writes=(dst_res,), tag=tag + "c")

    def bnd_row(self, hh):
        if hh == 0:
            return self.ROWSB[32:33, 0:2048]
        return self.ROWS[:, 1024:2048].bitcast(BF16)[32:33, 0:2048]

    def _bound(self, hh, qh, kap, kres, pts, name):
        P = self.P
        onb = self.cb_map["ones"]
        sq, rsq = pts[0]
        sk, rsk = pts[-1]
        rsq_t = rsq if isinstance(rsq, tuple) else (rsq,)
        rsk_t = rsk if isinstance(rsk, tuple) else (rsk,)
        r0, r1 = hh * 64, (hh + 1) * 64
        sqv = sq[r0:r1, 0:2048]
        skv = sk[r0:r1, 0:2048]
        P.emit("pool", lambda eng, skv=skv, kap=kap: eng.tensor_mul(skv, kap, kap),
               reads=kres, writes=rsk_t, tag=name + "_bk2")
        P.emit("pool", lambda eng, sqv=sqv, qh=qh: eng.tensor_mul(sqv, qh, qh),
               reads=(self.r_att[0],), writes=rsq_t, tag=name + "_bq2")
        ones64 = self.CB[r0:r1, onb + 128:onb + 129]
        for sg in range(4):
            P.emit("pe", lambda eng, sg=sg, skv=skv, ones64=ones64:
                   eng.matmul(self.PS[0:1, 2048 + sg * 512:2048 + sg * 512 + 512],
                              ones64, skv[:, sg * 512:(sg + 1) * 512],
                              start=True, stop=True),
                   reads=rsk_t + (self.r_cols,), writes=(self.r_bank[4 + sg],),
                   tag=name + "_bks")
        for sg in range(4):
            P.emit("pe", lambda eng, sg=sg, sqv=sqv, ones64=ones64:
                   eng.matmul(self.PS[0:1, sg * 512:sg * 512 + 512],
                              ones64, sqv[:, sg * 512:(sg + 1) * 512],
                              start=True, stop=True),
                   reads=rsq_t + (self.r_cols,), writes=(self.r_bank[sg],),
                   tag=name + "_bqs")
        P.emit("dve", lambda eng:
               eng.reduce_max(self.COLS[0:1, 372:373], self.PS[0:1, 2048:4096],
                              AX.X),
               reads=tuple(self.r_bank[4:8]), writes=(self.r_bnd,), hazard=True,
               tag=name + "_bmax")
        P.emit("act", lambda eng:
               eng.activation(self.COLS[0:1, 373:374], self.COLS[0:1, 372:373],
                              AF.Sqrt),
               reads=(self.r_bnd,), writes=(self.r_bnd,), tag=name + "_bsq")
        br = self.bnd_row(hh)
        P.emit("act", lambda eng, br=br:
               eng.activation(br, self.PS[0:1, 0:2048], AF.Sqrt),
               reads=tuple(self.r_bank[0:4]), writes=(self.r_bnd,),
               tag=name + "_bqr")
        P.emit("dve", lambda eng, br=br:
               eng.tensor_scalar(br, br, self.COLS[0:1, 373:374], 1.03,
                                 ALU.mult, ALU.mult),
               reads=(self.r_bnd,), writes=(self.r_bnd,), tag=name + "_brow")

    def _head(self, b, H, qh, kap, kres, kind, softcap, gw, n_kv, pts, name):
        P = self.P
        onb = self.cb_map["ones"]
        ngo = self.cb_map["negones"]
        mc_causal = self.cb_map["c_mask_causal"]
        mc_wind = self.cb_map["c_mask_wind"]
        hh = H % 2
        if kind == "causal":
            kts = [(kt, kt * 128, S) for kt in range(16)]
        elif kind == "full":
            kts = [(kt, 0, S) for kt in range(16)]
        else:
            kts = [(kt, kt * 128, min(kt * 128 + 1152, S)) for kt in range(16)]
        first_touch = [True] * 4
        last_touch = [None] * 4
        for kt, q0, q1 in kts:
            for sg in range(q0 // 512, cdiv(q1, 512)):
                last_touch[sg] = kt
        rot = 0
        pend_av = []
        for kt, q0, q1 in kts:
            pT, rpt = pts[kt % len(pts)]
            w = q1 - q0
            for sg in range(cdiv(w, 512)):
                lo, hi = sg * 512, min((sg + 1) * 512, w)
                bk = 4 + (rot + sg) % 4
                pc0 = bk * 512
                P.emit("pe", lambda eng, kt=kt, lo=lo, hi=hi, pc0=pc0, q0=q0:
                       eng.matmul(self.PS[:, pc0:pc0 + (hi - lo)],
                                  kap[:, kt * 128:(kt + 1) * 128],
                                  qh[:, q0 + lo:q0 + hi],
                                  start=True, stop=(softcap is not None)),
                       reads=kres + (self.r_att[0],), writes=(self.r_bank[bk],),
                       tag=name + "_sT")
                if softcap is None:
                    P.emit("pe", lambda eng, lo=lo, hi=hi, pc0=pc0, q0=q0,
                           br=self.bnd_row(hh):
                           eng.matmul(self.PS[:, pc0:pc0 + (hi - lo)],
                                      self.CB[32:33, ngo:ngo + 128],
                                      br[:, q0 + lo:q0 + hi],
                                      start=False, stop=True),
                           reads=(self.r_bnd, self.r_cols), writes=(self.r_bank[bk],),
                           tag=name + "_sTm")
                    P.emit("act", lambda eng, lo=lo, hi=hi, pc0=pc0, q0=q0, pT=pT:
                           eng.activation(pT[:, q0 + lo:q0 + hi],
                                          self.PS[:, pc0:pc0 + (hi - lo)],
                                          AF.Exp),
                           reads=(self.r_bank[bk],), writes=rpt,
                           tag=name + "_exp")
                else:
                    P.emit("act", lambda eng, lo=lo, hi=hi, pc0=pc0:
                           eng.activation(self.SC32[:, 0:hi - lo],
                                          self.PS[:, pc0:pc0 + (hi - lo)],
                                          AF.Tanh, scale=1.0 / softcap),
                           reads=(self.r_bank[bk],), writes=(self.r_sc32,),
                           tag=name + "_tanh")
                    P.emit("act", lambda eng, lo=lo, hi=hi, q0=q0, pT=pT:
                           eng.activation(pT[:, q0 + lo:q0 + hi],
                                          self.SC32[:, 0:hi - lo],
                                          AF.Exp, scale=float(softcap), bias=self.CF[:, 131:132]),
                           reads=(self.r_sc32,), writes=rpt, tag=name + "_exp")
            rot += cdiv(w, 512)
            if self.debug and H == 0 and kt == 0:
                self.dump(name + "_pT00", pT, rpt, bf16=True)
            if kind in ("causal", "window"):
                P.emit("pool", lambda eng, kt=kt, pT=pT:
                       eng.tensor_mul(pT[:, kt * 128:(kt + 1) * 128],
                                      pT[:, kt * 128:(kt + 1) * 128],
                                      self.CB[:, mc_causal:mc_causal + 128]),
                       reads=rpt + (self.r_cols,), writes=rpt, tag=name + "_msk")
            if kind == "window" and kt + 8 <= 15:
                P.emit("pool", lambda eng, kt=kt, pT=pT:
                       eng.tensor_mul(pT[:, (kt + 8) * 128:(kt + 9) * 128],
                                      pT[:, (kt + 8) * 128:(kt + 9) * 128],
                                      self.CB[:, mc_wind:mc_wind + 128]),
                       reads=rpt + (self.r_cols,), writes=rpt, tag=name + "_msk2")
            kvh = H // (16 // n_kv)
            vg = self.BB2[:, kt * gw + kvh * 65:kt * gw + kvh * 65 + 65]

            def av_emit(kt=kt, q0=q0, q1=q1, pT=pT, rpt=rpt, vg=vg):
                for sg in range(q0 // 512, cdiv(q1, 512)):
                    lo = max(q0, sg * 512)
                    hi = min(q1, (sg + 1) * 512)
                    st_flag = first_touch[sg]
                    first_touch[sg] = False
                    sp_flag = (last_touch[sg] == kt)
                    P.emit("pe", lambda eng, lo=lo, hi=hi, sg=sg, st=st_flag,
                           sp=sp_flag, vg=vg, pT=pT:
                           eng.matmul(self.PS[0:65, lo:hi], vg, pT[:, lo:hi],
                                      start=st, stop=sp),
                           reads=(self.r_bb2,) + rpt, writes=(self.r_bank[sg],),
                           tag=name + "_av")
            pend_av.append(av_emit)
            if len(pend_av) > 1:
                pend_av.pop(0)()
        for f in pend_av:
            f()
        def _linv(eng, lo, hi):
            with self.nc.allow_low_precision(reason="1/l in bf16 is within tolerance"):
                return eng.reciprocal(self.ROWSB[64:65, lo:hi],
                                      self.PS[64:65, lo:hi])
        P.emit("dve", lambda eng: _linv(eng, 0, 1024),
               reads=(self.r_bank[0], self.r_bank[1]), writes=(self.r_rowsb,),
               hazard=True, tag=name + "_linv")
        P.emit("dve", lambda eng: _linv(eng, 1024, 2048),
               reads=(self.r_bank[2], self.r_bank[3]), writes=(self.r_linv2,),
               hazard=True, tag=name + "_linv2")
        for sg in range(4):
            rr = self.r_rowsb if sg < 2 else self.r_linv2
            P.emit("pe", lambda eng, sg=sg:
                   eng.matmul(self.PS[0:64, 2048 + sg * 512:2048 + (sg + 1) * 512],
                              self.CB[64:65, onb:onb + 64],
                              self.ROWSB[64:65, sg * 512:(sg + 1) * 512],
                              start=True, stop=True),
                   reads=(rr,), writes=(self.r_bank[4 + sg],),
                   tag=name + "_lb")
            P.emit("act", lambda eng, sg=sg:
                   eng.activation(self.SC32[0:64, 0:512],
                                  self.PS[0:64, 2048 + sg * 512:2048 + (sg + 1) * 512],
                                  AF.Copy),
                   reads=(self.r_bank[4 + sg],), writes=(self.r_sc32,),
                   tag=name + "_lev")
            c = H // 2
            rowlo = (H % 2) * 64
            P.emit("dve", lambda eng, sg=sg, c=c, rowlo=rowlo:
                   eng.tensor_mul(self.BB1[rowlo:rowlo + 64,
                                           c * S + sg * 512:c * S + (sg + 1) * 512],
                                  self.PS[0:64, sg * 512:(sg + 1) * 512],
                                  self.SC32[0:64, 0:512]),
                   reads=(self.r_bank[sg], self.r_sc32), writes=(self.r_bb1,),
                   tag=name + "_out")

    def step_attn_a(self, b):
        self.phase_fence()
        self.ln_pre(b)
        for nm, off, sl in (("c_M1", 0, (0, 1)), ("c_M2", 2048, (2, 3))):
            self.P.dma("qsp", lambda eng, nm=nm, off=off:
                       eng.dma_start(self.WS[:, off:off + 2048], self.wbf[nm]),
                       reads=(self.r_wbf[nm],),
                       writes=tuple(self.r_ws[s] for s in sl), tag="m12")
        self.attn_sm(b, "w_a_q", "w_a_k", "w_a_v", "w_a_o", n_kv=16, rope=True,
                     qscale=0.125, kind="causal", name="a0")

    def step_attn_b(self, b):
        self.phase_fence()
        self.ln_pre(b)
        self.attn_sm(b, "w_b_q", "w_b_k", "w_b_v", "w_b_p", n_kv=16, rope=False,
                     qscale=0.125, kind="full", out_bias="bp", name="b0")

    def step_l1_attn_b(self, b):
        self.phase_fence()
        self.ln_pre(b)
        self.attn_sm(b, "w_lb_q", "w_lb_k", "w_lb_v", "w_lb_o", n_kv=4, rope=False,
                     qscale=0.125, kind="window", softcap=30.0, name="lb")

    @staticmethod
    def pick_slot(kcs, mc):
        ns = max(1, kcs // 8)
        if ns >= 4:
            return 0
        if ns == 2:
            return (0, 2)[mc % 2]
        return (0, 1, 2, 3)[mc % 4]

    def dense_layer(self, wname, n_mc, rhs_chunks, rhs_res, kcs, t0, width, evac_fn,
                    slots=(0, 1, 2, 3), tag="dl"):
        P = self.P
        nseg = cdiv(width, 512)
        for mc in range(n_mc):
            strip, sres = self.load_strip(wname, mc, self.pick_slot(kcs, mc))
            for sg in range(nseg):
                lo, hi = sg * 512, min((sg + 1) * 512, width)
                bank = (mc * nseg + sg) % 4
                for kc in range(kcs):
                    P.emit("pe", lambda eng, kc=kc, lo=lo, hi=hi, bank=bank,
                           strip=strip:
                           eng.matmul(self.PS[:, bank * 512:bank * 512 + (hi - lo)],
                                      strip[:, kc * 128:(kc + 1) * 128],
                                      rhs_chunks[kc][:, t0 + lo:t0 + hi],
                                      start=(kc == 0), stop=(kc == kcs - 1)),
                           reads=(sres[0], rhs_res[kc]), writes=(self.r_bank[bank],),
                           tag=tag)
                evac_fn(mc, sg, bank, lo, hi)

    def bb_chunk(self, i):
        if i < 16:
            return self.BB1[:, i * 1024:(i + 1) * 1024], self.r_bb1
        return self.BB2[:, (i - 16) * 1024:(i - 15) * 1024], self.r_bb2

    def router_l0(self, name="rt"):
        P = self.P
        onb = self.cb_map["ones"]
        ones4 = self.CB[0:4, onb:onb + 1]
        ones14 = self.CB[0:1, onb:onb + 4]
        regA = self.ATT[:, 0:2048]
        regB = self.ATT[:, 2048:4096]
        rA, rB = self.r_att[0], self.r_att[1]
        banks03 = tuple(self.r_bank[0:4])
        banks47 = tuple(self.r_bank[4:8])
        stripf = self.WS[:, 4096:6144].bitcast(F32)
        src_w = self.wf32["w_f_router"][0].rearrange("p k c -> p (k c)")
        P.dma("qsp", lambda eng: eng.dma_start(stripf, src_w),
              reads=(), writes=(self.r_ws[4], self.r_ws[5]), tag=name + "_wf")
        for kc in range(8):
            P.emit("pe", lambda eng, kc=kc:
                   eng.matmul(self.PS[0:4, 0:1], stripf[:, kc * 128:kc * 128 + 4],
                              self.CF[:, 128:129], start=(kc == 0), stop=(kc == 7)),
                   reads=(self.r_ws[4], self.r_ws[5], self.r_cols),
                   writes=(self.r_bank[0],), tag=name + "_cs")
        P.emit("dve", lambda eng:
               eng.tensor_scalar_mul(self.COLS[0:4, 378:379], self.PS[0:4, 0:1],
                                     -1.0),
               reads=(self.r_bank[0],), writes=(self.r_cols,), tag=name + "_csn")
        P.emit("pe", lambda eng:
               eng.transpose(self.PS[0:1, 0:4], self.COLS[0:4, 378:379],
                             self.CF[0:4, 0:4]),
               reads=(self.r_cols,), writes=(self.r_bank[0],), tag=name + "_csT")
        P.emit("dve", lambda eng:
               eng.tensor_copy(self.COLS[0:1, 374:378], self.PS[0:1, 0:4]),
               reads=(self.r_bank[0],), writes=(self.r_cols,), tag=name + "_csr")
        xr_chunks = [self.XR[:, c * S:(c + 1) * S] for c in range(8)]
        for sg in range(4):
            for kc in range(8):
                P.emit("pe", lambda eng, sg=sg, kc=kc:
                       eng.matmul(self.PS[0:4, sg * 512:(sg + 1) * 512],
                                  stripf[:, kc * 128:kc * 128 + 4],
                                  xr_chunks[kc][:, sg * 512:(sg + 1) * 512],
                                  start=(kc == 0), stop=False),
                       reads=(self.r_ws[4], self.r_ws[5], self.r_xr[kc]),
                       writes=(self.r_bank[sg],), tag=name + "_lg")
            P.emit("pe", lambda eng, sg=sg:
                   eng.matmul(self.PS[0:4, sg * 512:(sg + 1) * 512],
                              self.COLS[0:1, 374:378],
                              self.ROWS[0:1, sg * 512:(sg + 1) * 512],
                              start=False, stop=True),
                   reads=(self.r_cols, self.r_rows), writes=(self.r_bank[sg],),
                   tag=name + "_lgm")
        P.emit("dve", lambda eng:
               eng.tensor_copy(regB[0:4, :].bitcast(F32)[:, 0:1024]
                               if False else self.ROWS[0:4, 0:2048],
                               self.PS[0:4, 0:2048]),
               reads=banks03 + (self.r_rows,), writes=(self.r_rows,),
               tag=name + "_L0")
        P.emit("dve", lambda eng:
               eng.tensor_mul(self.ROWS[0:4, 0:2048], self.ROWS[0:4, 0:2048],
                              self.PS[0:4, 2048:4096]),
               reads=(self.r_rows,) + banks47, writes=(self.r_rows,),
               tag=name + "_Lr")
        P.emit("dve", lambda eng:
               eng.tensor_scalar_add(self.ROWS[0:4, 0:2048], self.ROWS[0:4, 0:2048],
                                     self.COLS[0:4, self.col_map["rblb"][0]:
                                               self.col_map["rblb"][0] + 1]),
               reads=(self.r_rows, self.r_cols), writes=(self.r_rows,),
               tag=name + "_L")
        for sg in range(4):
            P.emit("pe", lambda eng, sg=sg:
                   eng.matmul(self.PS[0:12, 2048 + sg * 512:2048 + (sg + 1) * 512],
                              self.CF[0:4, 134:146],
                              self.ROWS[0:4, sg * 512:(sg + 1) * 512],
                              start=True, stop=True),
                   reads=(self.r_rows, self.r_cols), writes=(self.r_bank[4 + sg],),
                   tag=name + "_D")
        P.emit("dve", lambda eng:
               eng.tensor_single_scalar(regB[0:12, :], self.PS[0:12, 2048:4096],
                                        0.0, ALU.is_gt),
               reads=banks47, writes=(rB,), tag=name + "_G")
        amo = self.cb_map["c_Amat"]
        for sg in range(4):
            P.emit("pe", lambda eng, sg=sg:
                   eng.matmul(self.PS[0:4, sg * 512:(sg + 1) * 512],
                              self.CB[0:12, amo:amo + 4], regB[0:12, sg * 512:(sg + 1) * 512],
                              start=True, stop=True),
                   reads=(rB, self.r_cols), writes=(self.r_bank[sg],), tag=name + "_R")
        P.emit("dve", lambda eng:
               eng.tensor_single_scalar(regB[0:4, :], self.PS[0:4, 0:2048],
                                        0.75, ALU.is_lt),
               reads=banks03, writes=(rB,), tag=name + "_ts")
        P.emit("dve", lambda eng:
               eng.tensor_mul(regB[0:4, :], self.ROWS[0:4, 0:2048], regB[0:4, :]),
               reads=(self.r_rows, rB), writes=(rB,), tag=name + "_T0")
        for sg in range(4):
            P.emit("pe", lambda eng, sg=sg:
                   eng.matmul(self.PS[0:1, 2048 + sg * 512:2048 + (sg + 1) * 512],
                              ones4, regB[0:4, sg * 512:(sg + 1) * 512],
                              start=True, stop=True),
                   reads=(rB, self.r_cols), writes=(self.r_bank[4 + sg],),
                   tag=name + "_sh")
        P.emit("dve", lambda eng:
               eng.tensor_single_scalar(regB[0:4, :], self.PS[0:4, 0:2048],
                                        1.5, ALU.is_lt),
               reads=banks03, writes=(rB,), tag=name + "_m01")
        P.emit("dve", lambda eng:
               eng.tensor_copy(regA[0:1, :], self.PS[0:1, 2048:4096]),
               reads=banks47, writes=(rA,), tag=name + "_shb")
        for sg in range(4):
            P.emit("pe", lambda eng, sg=sg:
                   eng.matmul(self.PS[0:4, sg * 512:(sg + 1) * 512],
                              ones14, regA[0:1, sg * 512:(sg + 1) * 512],
                              start=True, stop=True),
                   reads=(rA, self.r_cols), writes=(self.r_bank[sg],),
                   tag=name + "_shB")
        P.emit("dve", lambda eng:
               eng.tensor_sub(self.ROWS[0:4, 0:2048], self.ROWS[0:4, 0:2048],
                              self.PS[0:4, 0:2048]),
               reads=(self.r_rows,) + banks03, writes=(self.r_rows,), tag=name + "_Ls")
        P.emit("dve", lambda eng:
               eng.tensor_scalar_min(self.ROWS[0:4, 0:2048], self.ROWS[0:4, 0:2048],
                                     60.0),
               reads=(self.r_rows,), writes=(self.r_rows,))
        P.emit("act", lambda eng:
               eng.activation(self.ROWS[0:4, 0:2048], self.ROWS[0:4, 0:2048], AF.Exp),
               reads=(self.r_rows,), writes=(self.r_rows,), tag=name + "_E")
        P.emit("dve", lambda eng:
               eng.tensor_mul(self.ROWS[0:4, 0:2048], self.ROWS[0:4, 0:2048],
                              regB[0:4, :]),
               reads=(self.r_rows, rB), writes=(self.r_rows,), tag=name + "_EM")
        P.emit("dve", lambda eng:
               eng.tensor_copy(regB[0:4, :], self.ROWS[0:4, 0:2048]),
               reads=(self.r_rows,), writes=(rB,), tag=name + "_EMb")
        for sg in range(4):
            P.emit("pe", lambda eng, sg=sg:
                   eng.matmul(self.PS[0:1, 2048 + sg * 512:2048 + (sg + 1) * 512],
                              ones4, regB[0:4, sg * 512:(sg + 1) * 512],
                              start=True, stop=True),
                   reads=(rB, self.r_cols), writes=(self.r_bank[4 + sg],),
                   tag=name + "_dn")

        def _drb(eng):
            with self.nc.allow_low_precision(reason="router 1/denom bf16"):
                return eng.reciprocal(regA[0:1, :], self.PS[0:1, 2048:4096])
        P.emit("dve", _drb, reads=banks47, writes=(rA,), hazard=True,
               tag=name + "_dr")
        for sg in range(4):
            P.emit("pe", lambda eng, sg=sg:
                   eng.matmul(self.PS[0:4, sg * 512:(sg + 1) * 512],
                              ones14, regA[0:1, sg * 512:(sg + 1) * 512],
                              start=True, stop=True),
                   reads=(rA, self.r_cols), writes=(self.r_bank[sg],),
                   tag=name + "_drB")
        P.emit("dve", lambda eng:
               eng.tensor_mul(self.ROWS[0:4, 0:2048], self.ROWS[0:4, 0:2048],
                              self.PS[0:4, 0:2048]),
               reads=(self.r_rows,) + banks03, writes=(self.r_rows,), tag=name + "_W4")
        P.emit("dve", lambda eng:
               eng.tensor_copy(regA[0:4, :], self.ROWS[0:4, 0:2048]),
               reads=(self.r_rows,), writes=(rA,), tag=name + "_W4b")
        self.dump(name + "_W4", self.ROWS[0:4, 0:2048], self.r_rows)

    def wb_bcast(self, e, t0, name="wb"):
        P = self.P
        slo = self.cb_map["c_sel4"] + e * 128
        dst = self.ATT[:, 4096:5120]
        rwb = self.r_wb
        for sg in range(2):
            P.emit("pe", lambda eng, sg=sg:
                   eng.matmul(self.PS[:, 2048 + sg * 512:2048 + (sg + 1) * 512],
                              self.CB[0:4, slo:slo + 128],
                              self.ATT[0:4, t0 + sg * 512:t0 + (sg + 1) * 512],
                              start=True, stop=True),
                   reads=(self.r_att[0], self.r_cols), writes=(self.r_bank[4 + sg],),
                   tag=name)
            P.emit("act", lambda eng, sg=sg:
                   eng.activation(dst[:, sg * 512:(sg + 1) * 512],
                                  self.PS[:, 2048 + sg * 512:2048 + (sg + 1) * 512],
                                  AF.Copy),
                   reads=(self.r_bank[4 + sg],), writes=(rwb,),
                   tag=name + "_ev")
        return dst, rwb

    def final_into_xr(self, wname, rhs_chunks, rhs_res, kcs, t0, width, wb, rwb,
                      bias=None, post_scale=None, tag="fin"):
        P = self.P
        tmp = [(self.ATT[:, 5120:5632], self.r_tmp0),
               (self.ATT[:, 5632:6144], self.r_tmp1)]
        for mc in range(8):
            if kcs == 32:
                parts = []
                for h in range(2):
                    pidx = (2 * mc + h) % 3
                    parts.append(self.load_strip_part(
                        wname, mc, h * 16, (h + 1) * 16, (0, 2, 4)[pidx]))

                def lsel(kc, parts=parts):
                    ap, sl = parts[kc // 16]
                    kk = kc % 16
                    return ap[:, kk * 128:(kk + 1) * 128], sl[kk // 8]
            else:
                strip, sres = self.load_strip(wname, mc, self.pick_slot(kcs, mc))

                def lsel(kc, strip=strip, sres=sres):
                    return strip[:, kc * 128:(kc + 1) * 128], sres[0]
            for sg in range(cdiv(width, 512)):
                lo, hi = sg * 512, min((sg + 1) * 512, width)
                bank = (mc * 2 + sg) % 4
                for kc in range(kcs):
                    lhs, lres = lsel(kc)
                    P.emit("pe", lambda eng, kc=kc, lo=lo, hi=hi, bank=bank,
                           lhs=lhs:
                           eng.matmul(self.PS[:, bank * 512:bank * 512 + (hi - lo)],
                                      lhs,
                                      rhs_chunks[kc][:, lo:hi],
                                      start=(kc == 0), stop=(kc == kcs - 1)),
                           reads=(lres, rhs_res[kc]), writes=(self.r_bank[bank],),
                           tag=tag)
                tb, rtb = tmp[(mc * 2 + sg) % 2]
                if wb is None:
                    P.emit("dve", lambda eng, mc=mc, lo=lo, hi=hi, bank=bank, tb=tb:
                           eng.tensor_scalar(
                               tb[:, 0:hi - lo],
                               self.PS[:, bank * 512:bank * 512 + (hi - lo)],
                               self.col(bias, mc), post_scale, ALU.add, ALU.mult),
                           reads=(self.r_bank[bank], self.r_cols),
                           writes=(rtb,), tag=tag + "_w")
                elif bias is not None:
                    P.emit("dve", lambda eng, mc=mc, lo=lo, hi=hi, bank=bank,
                           tb=tb, wb=wb:
                           eng.scalar_tensor_tensor(
                               tb[:, 0:hi - lo], self.PS[:, bank * 512:bank * 512 + (hi - lo)],
                               self.col(bias, mc), wb[:, lo:hi], ALU.add, ALU.mult),
                           reads=(self.r_bank[bank], self.r_cols, rwb),
                           writes=(rtb,), tag=tag + "_w")
                else:
                    P.emit("dve", lambda eng, lo=lo, hi=hi, bank=bank, tb=tb, wb=wb:
                           eng.tensor_mul(tb[:, 0:hi - lo],
                                          self.PS[:, bank * 512:bank * 512 + (hi - lo)],
                                          wb[:, lo:hi]),
                           reads=(self.r_bank[bank], rwb), writes=(rtb,), tag=tag + "_w")
                xr = self.XR[:, mc * S + t0 + lo:mc * S + t0 + hi]
                P.emit("pool", lambda eng, xr=xr, tb=tb, lo=lo, hi=hi:
                       eng.tensor_add(xr, xr, tb[:, 0:hi - lo]),
                       reads=(rtb, self.r_xr[mc]), writes=(self.r_xr[mc],),
                       tag=tag + "_xr")

    def step_ffn0(self, b, inv=0):
        P = self.P
        self.phase_fence()
        self._keep_m = True
        self.ln_pre(b)
        self._keep_m = False
        self.router_l0(name=f"rt{inv}")
        xn = [self.XN[:, c * S:(c + 1) * S] for c in range(8)]
        for tt in range(2):
            t0 = tt * 1024
            xnt = [ap[:, t0:t0 + 1024] for ap in xn]
            wb, rwb = self.wb_bcast(3, t0)
            h_res = [None] * 32

            def ev_e3(mc, sg, bank, lo, hi):
                dst, rr = self.bb_chunk(mc)
                h_res[mc] = rr
                P.emit("act", lambda eng, dst=dst, bank=bank, lo=lo, hi=hi, mc=mc:
                       eng.activation(dst[:, lo:hi],
                                      self.PS[:, bank * 512:bank * 512 + (hi - lo)],
                                      AF.Gelu, bias=self.col("mlb1", mc)),
                       reads=(self.r_bank[bank], self.r_cols),
                       writes=(rr,), tag="e3_g")
            self.dense_layer("w_f_ml1", 32, xn, self.r_xn, 8, t0, 1024, ev_e3,
                             tag="e3")
            hch = [self.bb_chunk(i)[0] for i in range(32)]
            hrr = [self.bb_chunk(i)[1] for i in range(32)]
            self.final_into_xr("w_f_ml2", hch, hrr, 32, t0, 1024, wb, rwb,
                               bias="mlb2", tag="e3f")
            if tt == 0:
                self.dump(f"xr_e3_{inv}", self.XR[:, 0:2048], self.r_xr[0])
            wb, rwb = self.wb_bcast(1, t0)

            def mk_ev(dst_i0, biasname, func):
                def ev(mc, sg, bank, lo, hi):
                    dst, rr = self.bb_chunk(dst_i0 + mc)
                    P.emit("act", lambda eng, dst=dst, bank=bank, lo=lo, hi=hi, mc=mc:
                           eng.activation(dst[:, lo:hi],
                                          self.PS[:, bank * 512:bank * 512 + (hi - lo)],
                                          func, bias=self.col(biasname, mc)),
                           reads=(self.r_bank[bank], self.r_cols),
                           writes=(rr,), tag="ev")
                return ev
            ch = lambda i0, n: ([self.bb_chunk(i0 + i)[0] for i in range(n)],
                                [self.bb_chunk(i0 + i)[1] for i in range(n)])
            self.dense_layer("w_f_eq", 8, xn, self.r_xn, 8, t0, 1024,
                             mk_ev(0, "eqb", AF.Identity), tag="e1a")
            c0, r0 = ch(0, 8)
            self.dense_layer("w_f_mwv", 8, [a[:, 0:1024] for a in c0] if False else c0,
                             r0, 8, 0, 1024, mk_ev(8, "mbv", AF.Identity), tag="e1b")
            c1, r1 = ch(8, 8)
            self.dense_layer("w_f_mwo", 8, c1, r1, 8, 0, 1024,
                             mk_ev(0, "mbo", AF.Identity), tag="e1c")
            self.dense_layer("w_f_c1", 16, c0, r0, 8, 0, 1024,
                             mk_ev(16, "c1b", AF.Gelu), tag="e1d")
            c2, r2 = ch(16, 16)
            self.final_into_xr("w_f_c2", c2, r2, 16, t0, 1024, wb, rwb,
                               bias="c2b", tag="e1f")
            if tt == 0:
                self.dump(f"xr_e1_{inv}", self.XR[:, 0:2048], self.r_xr[0])
            wb, rwb = self.wb_bcast(2, t0)
            self.dense_layer("w_f_syn", 8, xn, self.r_xn, 8, t0, 1024,
                             mk_ev(0, "synb", AF.Identity), tag="e2a")
            self.dense_layer("w_f_cwv", 8, c0, r0, 8, 0, 1024,
                             mk_ev(8, "cbv", AF.Identity), tag="e2b")

            def ev_h2pre(mc, sg, bank, lo, hi):
                dst, rr = self.bb_chunk(16 + mc)
                syn_c, syn_r = self.bb_chunk(mc)
                P.emit("dve", lambda eng, dst=dst, bank=bank, lo=lo, hi=hi, mc=mc,
                       syn_c=syn_c:
                       eng.scalar_tensor_tensor(
                           dst[:, lo:hi], self.PS[:, bank * 512:bank * 512 + (hi - lo)],
                           self.col("cbo", mc), syn_c[:, lo:hi], ALU.add, ALU.add),
                       reads=(self.r_bank[bank], self.r_cols, syn_r),
                       writes=(rr,), tag="e2pre")
            self.dense_layer("w_f_cwo", 8, c1, r1, 8, 0, 1024, ev_h2pre, tag="e2c")
            c3, r3 = ch(16, 8)
            scr = (self.ATT[:, 6144:7168], self.r_att[3])
            self.ln(c3, c0, r3, r0, 1024, 8, gamma="n1g", beta="n1b",
                    in_f32=False, scratch=scr)
            self.dense_layer("w_f_f1", 16, c0, r0, 8, 0, 1024,
                             mk_ev(16, "f1b", AF.Relu), tag="e2d")

            def ev_h3pre(mc, sg, bank, lo, hi):
                dst, rr = self.bb_chunk(8 + mc)
                h2_c, h2_r = self.bb_chunk(mc)
                P.emit("dve", lambda eng, dst=dst, bank=bank, lo=lo, hi=hi, mc=mc,
                       h2_c=h2_c:
                       eng.scalar_tensor_tensor(
                           dst[:, lo:hi], self.PS[:, bank * 512:bank * 512 + (hi - lo)],
                           self.col("f2b", mc), h2_c[:, lo:hi], ALU.add, ALU.add),
                       reads=(self.r_bank[bank], self.r_cols, h2_r),
                       writes=(rr,), tag="e2pre2")
            self.dense_layer("w_f_f2", 8, c2, r2, 16, 0, 1024, ev_h3pre, tag="e2e")
            c4, r4 = ch(8, 8)
            self.ln(c4, c4, r4, r4, 1024, 8, gamma="n2g", beta="n2b",
                    in_f32=False, scratch=scr)
            self.final_into_xr("w_f_gen", c4, r4, 8, t0, 1024, wb, rwb,
                               bias="genb", tag="e2f")
            if tt == 0:
                self.dump(f"xr_e2_{inv}", self.XR[:, 0:2048], self.r_xr[0])
            wb, rwb = self.wb_bcast(0, t0)
            sil = self.ATT[:, 6144:7168]
            rsil = self.r_att[3]

            def ev_sil(mc, sg, bank, lo, hi):
                P.emit("act", lambda eng, bank=bank, lo=lo, hi=hi:
                       eng.activation(sil[:, lo:hi],
                                      self.PS[:, bank * 512:bank * 512 + (hi - lo)],
                                      AF.Silu),
                       reads=(self.r_bank[bank],), writes=(rsil,), tag="e0s")

            def ev_g(mc, sg, bank, lo, hi):
                dst, rr = self.bb_chunk(mc)
                P.emit("dve", lambda eng, dst=dst, bank=bank, lo=lo, hi=hi:
                       eng.tensor_mul(dst[:, lo:hi], sil[:, lo:hi],
                                      self.PS[:, bank * 512:bank * 512 + (hi - lo)]),
                       reads=(self.r_bank[bank], rsil), writes=(rr,), tag="e0g")
            for mc in range(32):
                s1, sr1 = self.load_strip("w_f_sw1", mc, (0, 1)[mc % 2])
                s3, sr3 = self.load_strip("w_f_sw3", mc, (2, 3)[mc % 2])
                for sg in range(2):
                    lo, hi = sg * 512, (sg + 1) * 512
                    bank = sg
                    for kc in range(8):
                        P.emit("pe", lambda eng, kc=kc, lo=lo, hi=hi, bank=bank,
                               s1=s1, rhs=xnt[kc]:
                               eng.matmul(self.PS[:, bank * 512:bank * 512 + 512],
                                          s1[:, kc * 128:(kc + 1) * 128],
                                          rhs[:, lo:hi],
                                          start=(kc == 0), stop=(kc == 7)),
                               reads=(sr1[0], self.r_xn[kc]),
                               writes=(self.r_bank[bank],), tag="e0w1")
                    ev_sil(mc, sg, bank, lo, hi)
                    for kc in range(8):
                        P.emit("pe", lambda eng, kc=kc, lo=lo, hi=hi, s3=s3,
                               rhs=xnt[kc]:
                               eng.matmul(self.PS[:, (2 + (lo // 512)) * 512:
                                                  (3 + (lo // 512)) * 512],
                                          s3[:, kc * 128:(kc + 1) * 128],
                                          rhs[:, lo:hi],
                                          start=(kc == 0), stop=(kc == 7)),
                               reads=(sr3[0], self.r_xn[kc]),
                               writes=(self.r_bank[2 + sg],), tag="e0w3")
                    ev_g(mc, sg, 2 + sg, lo, hi)
            gch = [self.bb_chunk(i)[0] for i in range(32)]
            grr = [self.bb_chunk(i)[1] for i in range(32)]
            if tt == 0:
                self.dump(f"g0_{inv}", gch[0], grr[0], bf16=True)
                self.dump(f"g20_{inv}", gch[20], grr[20], bf16=True)
                self.dump(f"wb0_{inv}", wb, rwb, bf16=True)
            self.final_into_xr("w_f_sw2", gch, grr, 32, t0, 1024, wb, rwb,
                               bias=None, tag="e0f")
            if tt == 0:
                self.dump(f"xr_e0_{inv}", self.XR[:, 0:2048], self.r_xr[0])


    def fm_chain(self, dst, dst_res, banks0, width, tag):
        P = self.P
        mx = self.ATT[:, 6144:6656]
        for sg in range(cdiv(width, 512)):
            lo, hi = sg * 512, min((sg + 1) * 512, width)
            bk = banks0 + sg
            P.emit("dve", lambda eng, lo=lo, hi=hi, bk=bk:
                   eng.tensor_scalar_min(self.SC32[:, 0:hi - lo],
                                         self.PS[:, bk * 512:bk * 512 + (hi - lo)],
                                         0.0),
                   reads=(self.r_bank[bk],), writes=(self.r_sc32,), tag=tag + "_mn")
            P.emit("act", lambda eng, lo=lo, hi=hi, dst=dst:
                   eng.activation(dst[:, lo:hi], self.SC32[:, 0:hi - lo], AF.Exp),
                   reads=(self.r_sc32,), writes=dst_res, tag=tag + "_ex")
            P.emit("act", lambda eng, lo=lo, hi=hi, bk=bk, mx=mx:
                   eng.activation(mx[:, 0:hi - lo],
                                  self.PS[:, bk * 512:bk * 512 + (hi - lo)],
                                  AF.Relu),
                   reads=(self.r_bank[bk],), writes=(self.r_att[3],), tag=tag + "_mx")
            P.emit("pool", lambda eng, lo=lo, hi=hi, dst=dst, mx=mx:
                   eng.tensor_add(dst[:, lo:hi], dst[:, lo:hi],
                                  mx[:, 0:hi - lo]),
                   reads=(self.r_att[3],) + dst_res, writes=dst_res, tag=tag + "_ad")

    def tok_transpose_bf(self, src_bf, src_res, dstf, dst_res, tag):
        P = self.P
        ib = self.cb_map["c_identb"]
        pb = self.PS[:, 3584:3648].bitcast(BF16)
        for st in range(16):
            P.emit("pe", lambda eng, st=st:
                   eng.transpose(pb, src_bf[:, st * 128:(st + 1) * 128],
                                 self.CB[:, ib:ib + 128]),
                   reads=src_res + (self.r_cols,), writes=(self.r_bank[7],),
                   tag=tag + "_t")
            dst = dstf(st)
            P.emit("dve", lambda eng, dst=dst:
                   eng.tensor_copy(dst, pb.rearrange("p (g c) -> p g c", g=2)),
                   reads=(self.r_bank[7],), writes=(dst_res,), tag=tag + "_e")

    def step_l1_attn_a(self, b):
        import os
        la_phase = int(os.environ.get("LA_PHASE", "4"))
        P = self.P
        self.phase_fence()
        self.ln_pre(b)
        xn = [self.XN[:, c * S:(c + 1) * S] for c in range(8)]
        stage = self.ATT[:, 4096:6144]
        for mc in range(16):
            strip, sres = self.load_strip("w_la_kv", mc, (4, 5)[mc % 2])
            self.mm_to_psum(strip, sres[0], xn, self.r_xn, 0, S, tag="la_kv")
            if mc < 8:
                self.fm_chain(stage, self.r_mid, 0, S, "la_fm")
            else:
                for sg in range(4):
                    P.emit("act", lambda eng, sg=sg:
                           eng.activation(stage[:, sg * 512:(sg + 1) * 512],
                                          self.PS[:, sg * 512:(sg + 1) * 512],
                                          AF.Copy),
                           reads=(self.r_bank[sg],), writes=self.r_mid,
                           tag="la_vev")
            if mc < 8:
                def dstf(st, mc=mc):
                    base = st * 1040 + (2 * mc) * 65
                    return self.BB2[:, base:base + 130].rearrange(
                        "p (g c) -> p g c", g=2)[:, :, 0:64]
                self.tok_transpose_bf(stage, self.r_mid, dstf, self.r_bb2, "la_kt")
            else:
                def dstf(st, mc=mc):
                    base = st * 1024 + (2 * (mc - 8)) * 64
                    return self.BB1[:, base:base + 128].rearrange(
                        "p (g c) -> p g c", g=2)
                self.tok_transpose_bf(stage, self.r_mid, dstf, self.r_bb1, "la_vt")
        if la_phase < 2:
            return
        P.emit("dve", lambda eng: eng.memset(
            self.BB2[:, 0:16 * 1040].rearrange("p (n c) -> p n c", c=65)[:, :, 64:65],
            1.0), reads=(), writes=(self.r_bb2,), tag="la_ones")
        for h in range(16):
            for nt in range(16):
                kg = self.BB2[:, nt * 1040 + h * 65:nt * 1040 + h * 65 + 65]
                vg = self.BB1[:, nt * 1024 + h * 64:nt * 1024 + h * 64 + 64]
                P.emit("pe", lambda eng, kg=kg, vg=vg, h=h, nt=nt:
                       eng.matmul(self.PS[0:65, h * 128:h * 128 + 64], kg, vg,
                                  start=(nt == 0), stop=(nt == 15)),
                       reads=(self.r_bb2, self.r_bb1), writes=(self.r_bank[h // 4],),
                       tag="la_kv2")
                P.emit("pe", lambda eng, kg=kg, h=h, nt=nt:
                       eng.matmul(self.PS[0:65, h * 128 + 64:h * 128 + 65], kg,
                                  self.CB[:, self.cb_map["ones"] + 128:
                                          self.cb_map["ones"] + 129],
                                  start=False, stop=(nt == 15)),
                       reads=(self.r_bb2, self.r_cols),
                       writes=(self.r_bank[h // 4],), tag="la_ks")
        kvb = self.ATT[:, 4096:6144]
        for h in range(16):
            base = (h % 2) * 64
            P.emit("dve", lambda eng, h=h, base=base:
                   eng.tensor_copy(kvb[base:base + 64, h * 128:h * 128 + 65],
                                   self.PS[0:64, h * 128:h * 128 + 65]),
                   reads=(self.r_bank[h // 4],), writes=self.r_mid, tag="la_kve")
        if la_phase < 3:
            return
        onb = self.cb_map["ones"]
        for p in range(8):
            qstrip, qres = self.load_strip("w_la_q", p, 4)
            self.mm_to_psum(qstrip, qres[0], xn, self.r_xn, 0, S, tag="la_q")
            qd = self.ATT[:, 0:2048]
            self.fm_chain(qd, (self.r_att[0],), 0, S, "la_qf")
            for hh in range(2):
                H = 2 * p + hh
                qh = qd[hh * 64:(hh + 1) * 64, :]
                base = hh * 64
                ksc = kvb[base:base + 64, H * 128 + 64:H * 128 + 65]
                for sg in range(4):
                    P.emit("pe", lambda eng, sg=sg, ksc=ksc, qh=qh:
                           eng.matmul(self.PS[0:1, 2048 + sg * 512:
                                              2048 + (sg + 1) * 512],
                                      ksc, qh[:, sg * 512:(sg + 1) * 512],
                                      start=True, stop=True),
                           reads=self.r_mid + (self.r_att[0],),
                           writes=(self.r_bank[4 + sg],), tag="la_z")
                P.emit("dve", lambda eng:
                       eng.tensor_scalar_add(self.ROWS[0:1, 0:2048],
                                             self.PS[0:1, 2048:4096], 1e-6),
                       reads=tuple(self.r_bank[4:8]), writes=(self.r_rows,),
                       tag="la_ze")

                def _zr(eng, lo, hi):
                    with self.nc.allow_low_precision(reason="1/z bf16 ok"):
                        return eng.reciprocal(self.ROWSB[0:1, lo:hi],
                                              self.ROWS[0:1, lo:hi])
                P.emit("dve", lambda eng: _zr(eng, 0, 1024),
                       reads=(self.r_rows,), writes=(self.r_rowsb,),
                       hazard=True, tag="la_zr")
                P.emit("dve", lambda eng: _zr(eng, 1024, 2048),
                       reads=(self.r_rows,), writes=(self.r_linv2,),
                       hazard=True, tag="la_zr2")
                kvh = kvb[base:base + 64, H * 128:H * 128 + 64]
                for sg in range(4):
                    P.emit("pe", lambda eng, sg=sg, kvh=kvh, qh=qh:
                           eng.matmul(self.PS[0:64, sg * 512:(sg + 1) * 512],
                                      kvh, qh[:, sg * 512:(sg + 1) * 512],
                                      start=True, stop=True),
                           reads=self.r_mid + (self.r_att[0],),
                           writes=(self.r_bank[sg],), tag="la_o")
                    P.emit("pe", lambda eng, sg=sg:
                           eng.matmul(self.PS[0:64, 2048 + sg * 512:
                                              2048 + (sg + 1) * 512],
                                      self.CB[0:1, onb:onb + 64],
                                      self.ROWSB[0:1, sg * 512:(sg + 1) * 512],
                                      start=True, stop=True),
                           reads=(self.r_rowsb if sg < 2 else self.r_linv2,
                                  self.r_cols),
                           writes=(self.r_bank[4 + sg],), tag="la_zb")
                    P.emit("act", lambda eng, sg=sg:
                           eng.activation(self.SC32[0:64, 0:512],
                                          self.PS[0:64, 2048 + sg * 512:
                                                  2048 + (sg + 1) * 512],
                                          AF.Copy),
                           reads=(self.r_bank[4 + sg],), writes=(self.r_sc32,),
                           tag="la_zbe")
                    c = H // 2
                    rowlo = (H % 2) * 64
                    P.emit("dve", lambda eng, sg=sg, c=c, rowlo=rowlo:
                           eng.tensor_mul(
                               self.BB1[rowlo:rowlo + 64,
                                        c * S + sg * 512:c * S + (sg + 1) * 512],
                               self.PS[0:64, sg * 512:(sg + 1) * 512],
                               self.SC32[0:64, 0:512]),
                           reads=(self.r_bank[sg], self.r_sc32),
                           writes=(self.r_bb1,), tag="la_out")
        if la_phase < 4:
            return
        bb1_chunks = [self.BB1[:, c * S:(c + 1) * S] for c in range(8)]
        self.proj_residual("w_la_p", bb1_chunks, [self.r_bb1] * 8, bias="bproj",
                           tag="la_wo")

    def step_l1_ffn(self, b):
        P = self.P
        self.phase_fence()
        self.ln_pre(b)
        xn = [self.XN[:, c * S:(c + 1) * S] for c in range(8)]
        for tt in range(2):
            t0 = tt * 1024

            def ev_s1(mc, sg, bank, lo, hi):
                dst, rr = self.bb_chunk(mc)
                P.emit("act", lambda eng, dst=dst, bank=bank, lo=lo, hi=hi, mc=mc:
                       eng.activation(dst[:, lo:hi],
                                      self.PS[:, bank * 512:bank * 512 + (hi - lo)],
                                      AF.Silu, bias=self.col("sb1", mc)),
                       reads=(self.r_bank[bank], self.r_cols),
                       writes=(rr,), tag="ls1")
            self.dense_layer("w_lf_s1", 16, xn, self.r_xn, 8, t0, 1024, ev_s1,
                             tag="ls1")

            def ev_s3(mc, sg, bank, lo, hi):
                dst, rr = self.bb_chunk(16 + mc)
                sc, srr = self.bb_chunk(mc)
                P.emit("dve", lambda eng, dst=dst, sc=sc, bank=bank, lo=lo, hi=hi,
                       mc=mc:
                       eng.scalar_tensor_tensor(
                           dst[:, lo:hi],
                           self.PS[:, bank * 512:bank * 512 + (hi - lo)],
                           self.col("sb3", mc), sc[:, lo:hi], ALU.add, ALU.mult),
                       reads=(self.r_bank[bank], self.r_cols, srr),
                       writes=(rr,), tag="ls3")
            self.dense_layer("w_lf_s3", 16, xn, self.r_xn, 8, t0, 1024, ev_s3,
                             tag="ls3")
            gch = [self.bb_chunk(16 + i)[0] for i in range(16)]
            grr = [self.bb_chunk(16 + i)[1] for i in range(16)]
            self.final_into_xr("w_lf_s2", gch, grr, 16, t0, 1024, None, None,
                               bias="sb2", post_scale=0.5, tag="ls2")
        self.dump("lf_shared", self.XR[:, 0:2048], self.r_xr[0])
        tmp = [(self.ATT[:, 5120:5632], self.r_tmp0),
               (self.ATT[:, 5632:6144], self.r_tmp1)]
        for r in range(8):
            xnr = [ap.rearrange("p (a b) -> p b a", b=8)[:, r, :] for ap in xn]
            for kk in range(2):
                e = (3 * r + 7 * kk) % 8
                for mc in range(8):
                    strip, sres = self.load_strip(f"w_lf_e1_{e}", mc, (4, 5)[mc % 2])
                    bank = mc % 2
                    for kc in range(8):
                        P.emit("pe", lambda eng, kc=kc, bank=bank, strip=strip,
                               rhs=xnr[kc]:
                               eng.matmul(self.PS[:, bank * 512:bank * 512 + 256],
                                          strip[:, kc * 128:(kc + 1) * 128], rhs,
                                          start=(kc == 0), stop=(kc == 7)),
                               reads=(sres[0], self.r_xn[kc]),
                               writes=(self.r_bank[bank],), tag="le1")
                    P.emit("act", lambda eng, mc=mc, bank=bank, e=e:
                           eng.activation(self.BB1[:, mc * 256:(mc + 1) * 256],
                                          self.PS[:, bank * 512:bank * 512 + 256],
                                          AF.Gelu, bias=self.col("lfb1", e * 8 + mc)),
                           reads=(self.r_bank[bank], self.r_cols),
                           writes=(self.r_bb1,), tag="le1g")
                for oc in range(8):
                    strip, sres = self.load_strip(f"w_lf_e2_{e}", oc, (4, 5)[oc % 2])
                    for kc in range(8):
                        P.emit("pe", lambda eng, kc=kc, oc=oc, strip=strip, kk=kk:
                               eng.matmul(self.PS[:, 2048 + oc * 256:
                                                  2048 + oc * 256 + 256],
                                          strip[:, kc * 128:(kc + 1) * 128],
                                          self.BB1[:, kc * 256:(kc + 1) * 256],
                                          start=(kk == 0 and kc == 0 and oc % 2 == 0),
                                          stop=(kk == 1 and kc == 7)),
                               reads=(sres[0], self.r_bb1),
                               writes=(self.r_bank[4 + oc // 2],), tag="le2")
            if r == 0:
                self.dump("lf_h", self.BB1[:, 0:2048], self.r_bb1, bf16=True)
            e0_, e1_ = (3 * r) % 8, (3 * r + 7) % 8
            a = self.col_map["lfb2c"][0]
            b2a = self.col_map["lfb2"][0]
            P.emit("dve", lambda eng, a=a, b2a=b2a, e0_=e0_, e1_=e1_:
                   eng.tensor_add(self.COLS[:, a:a + 8],
                                  self.COLS[:, b2a + e0_ * 8:b2a + e0_ * 8 + 8],
                                  self.COLS[:, b2a + e1_ * 8:b2a + e1_ * 8 + 8]),
                   reads=(self.r_cols,), writes=(self.r_cols,), tag="leb")
            for oc in range(8):
                tb, rtb = tmp[oc % 2]
                P.emit("dve", lambda eng, oc=oc, tb=tb, a=a:
                       eng.tensor_scalar(tb[:, 0:256],
                                         self.PS[:, 2048 + oc * 256:
                                                 2048 + oc * 256 + 256],
                                         self.COLS[:, a + oc:a + oc + 1], 0.25,
                                         ALU.add, ALU.mult),
                       reads=(self.r_bank[4 + oc // 2], self.r_cols),
                       writes=(rtb,), tag="lee")
                xrv = self.XR[:, oc * S:(oc + 1) * S].rearrange(
                    "p (a b) -> p b a", b=8)[:, r, :]
                P.emit("pool", lambda eng, xrv=xrv, tb=tb, oc=oc:
                       eng.tensor_add(xrv, xrv, tb[:, 0:256]),
                       reads=(rtb, self.r_xr[oc]), writes=(self.r_xr[oc],),
                       tag="lexr")
                if r == 0 and oc == 0:
                    self.dump("lf_tb0", tb[:, 0:256], rtb, bf16=True)
            if r == 0:
                self.dump("lf_ex0", self.XR[:, 0:2048], self.r_xr[0])

    def step_sequence(self):
        seq = [self.step_attn_a]
        for nm in ("step_ffn0", "step_attn_b", "step_ffn0", "step_l1_attn_a",
                   "step_l1_ffn", "step_l1_attn_b", "step_l1_ffn"):
            if hasattr(self, nm):
                seq.append(getattr(self, nm))
            else:
                break
        return seq


VEC_NAMES = [
    ("bp", "L0_attn_b_bp", 1024),
    ("rb", "L0_ffn_router_b", 4), ("lb", "L0_ffn_load_balancer", 4),
    ("eqb", "L0_ffn_me_eq_b", 1024), ("mbv", "L0_ffn_me_bv", 1024),
    ("mbo", "L0_ffn_me_bo", 1024), ("c1b", "L0_ffn_me_c1b", 2048),
    ("c2b", "L0_ffn_me_c2b", 1024), ("synb", "L0_ffn_ce_syn_b", 1024),
    ("cbv", "L0_ffn_ce_bv", 1024), ("cbo", "L0_ffn_ce_bo", 1024),
    ("n1g", "L0_ffn_ce_n1g", 1024), ("n1b", "L0_ffn_ce_n1b", 1024),
    ("f1b", "L0_ffn_ce_f1b", 2048), ("f2b", "L0_ffn_ce_f2b", 1024),
    ("n2g", "L0_ffn_ce_n2g", 1024), ("n2b", "L0_ffn_ce_n2b", 1024),
    ("genb", "L0_ffn_ce_gen_b", 1024), ("mlb1", "L0_ffn_ml_b1", 4096),
    ("mlb2", "L0_ffn_ml_b2", 1024),
    ("bproj", "L1_attn_a_bproj", 1024),
    ("sb1", "L1_ffn_sb1", 2048), ("sb3", "L1_ffn_sb3", 2048),
    ("sb2", "L1_ffn_sb2", 1024),
    ("lfb1", "L1_ffn_b1", 8192), ("lfb2", "L1_ffn_b2", 8192),
]


def build_program(gshapes, n_steps=8, nb=NB, debug=False):
    nc = bass.Bass("TRN2", target_bir_lowering=False)
    stack = ExitStack()
    with stack:
        bd = Build(nc, stack, gshapes, n_steps=n_steps, nb=nb)
        bd.debug = debug
        consts = host_consts()
        bd.prologue(consts, VEC_NAMES)
        for b in range(nb):
            bd.P.new_segment()
            bd.load_x(b)
            seq = bd.step_sequence()
            assert n_steps <= len(seq), f"only {len(seq)} steps implemented"
            for k in range(n_steps):
                seq[k](b)
            bd.store_x(b)
        bd.finish()
        bd.P.finalize(stack)
    return nc, bd


def _host_inputs(inputs, nb_list):
    g = host_prep(inputs)
    consts = host_consts()
    base = {}
    for k, v in g.items():
        base[k] = np.ascontiguousarray(v, dtype=np.float32)
    for k, v in consts.items():
        base[k] = np.ascontiguousarray(v, dtype=np.float32)
    for vn, src, length in VEC_NAMES:
        base["v_" + vn] = np.ascontiguousarray(
            inputs[src], dtype=np.float32).reshape(length)
    x = inputs["L0_attn_a_in_features"]
    maps = []
    for bl in nb_list:
        m = dict(base)
        m["x"] = np.ascontiguousarray(x[list(bl)], dtype=np.float32)
        maps.append(m)
    return maps, g


FUSED = True


def kernel(**inputs):
    x = inputs["L0_attn_a_in_features"]
    if FUSED:
        maps, g = _host_inputs(
            inputs, [range(c * NB, (c + 1) * NB) for c in range(N_CORES)])
        gshapes = {k: v.shape for k, v in g.items()}
        nc, _ = build_program(gshapes, n_steps=8, nb=NB)
        res = run_bass_kernel_spmd(nc, maps, core_ids=list(range(N_CORES)))
        out = np.empty((32, S, D), dtype=np.float32)
        for c in range(N_CORES):
            out[c * NB:(c + 1) * NB] = res.results[c]["y"]
        return out
    out = np.empty((32, S, D), dtype=np.float32)
    maps, g = _host_inputs(inputs, [range(c * NB, c * NB + 1) for c in range(N_CORES)])
    gshapes = {k: v.shape for k, v in g.items()}
    nc, _ = build_program(gshapes, n_steps=8, nb=1)
    for j in range(NB):
        for c in range(N_CORES):
            maps[c]["x"] = np.ascontiguousarray(x[c * NB + j:c * NB + j + 1],
                                                dtype=np.float32)
        res = run_bass_kernel_spmd(nc, maps, core_ids=list(range(N_CORES)))
        for c in range(N_CORES):
            out[c * NB + j] = res.results[c]["y"][0]
    return out
